# version 15
# baseline (speedup 1.0000x reference)
import contextlib
import math
import numpy as np

import concourse.bass as bass
import concourse.mybir as mybir
from concourse.bass_utils import run_bass_kernel_spmd

F32 = mybir.dt.float32
BF16 = mybir.dt.bfloat16
I32 = mybir.dt.int32
AF = mybir.ActivationFunctionType
ALU = mybir.AluOpType

B, N, D, H, DH, NPAIR = 4, 4096, 1024, 16, 64, 32
SLOTS = 4
EXT = [32, 24, 16, 8]
CHUNKS_ROLE = ([7, 5, 3, 1], [6, 4, 2, 0])
NG = 4
FCH = [704, 704, 640]
FCH_OFF = [0, 704, 1408]
NSUB = {704: [512, 192], 640: [512, 128]}

RIEMANN = (14.134725142, 21.022039639, 25.01085758, 30.424876126, 32.935061588,
           37.586178159, 40.918719012, 43.327073281, 48.005150881, 49.773832478,
           52.970321478, 56.446247697, 59.347044003, 60.831778525, 65.112544048,
           67.079810529, 69.546401711, 72.067157674, 75.704690699, 77.144840069,
           79.33737502, 82.910380854, 84.735492981, 87.425274613, 88.809111208,
           92.491899271, 94.651344041, 95.870634228, 98.831194218, 101.317851006,
           103.72553804, 105.446623052)

TWO_PI = 2.0 * math.pi
CW3 = float(np.float32(TWO_PI - 6.28125))
EXP_SHIFT = -32.0

ENGS = ("t", "s", "v", "y", "g")
SEMS = ("ds", "dg", "mt", "ms", "mv")

O_KF = 0
O_QF = 8192
O_V16 = 12288
O_XC = 20992
O_WQ = 29184
O_EXP = 35328
O_ATT = 37376
O_WO = 8192
F_H = 0
F_C = 22528
F_XN = 45056
F_W1P = 50688
F_W2P = 52736
F_W3P = 30720
F_HP = 54784
PH16_COLS = 62976

W_XO = 0
W_R = 4096
OV = 8192
A_ROPE = OV
A_PK = OV + 4608
A_UBWB = OV
W_RSQ = OV
W_XNF = OV + 2048
LN_M = OV + 3072
LN_V = OV + 3584
LN_T = OV + 4096
LN_S = OV + 4608
LN_RSTD = OV + 5120
PH32_COLS = 13824


def _invfreq():
    g = np.asarray(RIEMANN[:NPAIR], dtype=np.float32)
    return (np.float32(1.0) / (g / g[0])).astype(np.float32)


def _pm_tiles(a):
    K, M = a.shape
    return np.ascontiguousarray(
        a.reshape(K // 128, 128, M).transpose(1, 0, 2).reshape(128, -1))


def _panel_tiles(aT):
    K, O = aT.shape
    r = aT.reshape(K // 128, 128, O // 128, 128).transpose(1, 2, 0, 3)
    return np.ascontiguousarray(r.reshape(128, -1))


def prep_inputs(inp):
    x = np.asarray(inp["L0_attn_x"])
    w_qkv = np.asarray(inp["L0_attn_w_qkv"])
    w_o = np.asarray(inp["L0_attn_w_o"])
    ls = np.asarray(inp["L0_attn_log_scale"])
    w1 = np.asarray(inp["L0_ffn_w1_w"]); s1 = np.asarray(inp["L0_ffn_w1_s"])
    w2 = np.asarray(inp["L0_ffn_w2_w"]); s2 = np.asarray(inp["L0_ffn_w2_s"])
    h1 = np.asarray(inp["L0_ffn_h1_w"]); sh1 = np.asarray(inp["L0_ffn_h1_s"])
    h2 = np.asarray(inp["L0_ffn_h2_w"]); sh2 = np.asarray(inp["L0_ffn_h2_s"])
    w3 = np.asarray(inp["L0_ffn_w3_w"]); s3 = np.asarray(inp["L0_ffn_w3_s"])

    invf = _invfreq()
    perm = np.concatenate([np.arange(0, 32, 2), np.arange(1, 32, 2),
                           np.arange(32, 64, 2), np.arange(33, 64, 2)])

    cols = []
    for g in range(NG):
        qc, kc, vc = [], [], []
        for h in range(4 * g, 4 * g + 4):
            qc.extend((h * DH + perm).tolist())
            kc.extend((1024 + h * DH + perm).tolist())
            vc.extend(range(2048 + h * DH, 2048 + (h + 1) * DH))
        cols.extend(qc + kc + vc)
    wsel = w_qkv[:, cols]
    wqkvg = np.ascontiguousarray(
        wsel.reshape(8, 128, NG, 768).transpose(1, 2, 0, 3).reshape(128, -1))

    shared = dict(
        wqkvg=wqkvg,
        wo=_pm_tiles(w_o),
        w1=_panel_tiles(np.ascontiguousarray(w1.T)),
        w2=_panel_tiles(np.ascontiguousarray(w2.T)),
        h1=_panel_tiles(np.ascontiguousarray(h1.T)),
        h2=_panel_tiles(np.ascontiguousarray(h2.T)),
        w3=_panel_tiles(np.ascontiguousarray(w3.T)),
        s1=np.ascontiguousarray(s1.reshape(32, 128).T),
        s2=np.ascontiguousarray(s2.reshape(32, 128).T),
        sh1=np.ascontiguousarray(sh1.reshape(32, 128).T),
        sh2=np.ascontiguousarray(sh2.reshape(32, 128).T),
        s3=np.ascontiguousarray(s3.reshape(8, 128).T),
    )
    lsp = np.empty((128, 8), dtype=np.float32)
    for hp in range(8):
        lsp[0:64, hp] = ls[2 * hp]
        lsp[64:128, hp] = ls[2 * hp + 1]
    shared["lsp"] = lsp
    pp_ = np.arange(128)
    prow = ((pp_ % 64) // 32) * 16 + (pp_ % 16)
    pos_all = np.arange(N, dtype=np.float32)
    shared["pk"] = np.ascontiguousarray(
        (pos_all[None, :] * invf[prow][:, None]).astype(np.float32))
    shared["qpos"] = np.ascontiguousarray(
        np.broadcast_to(np.arange(512, dtype=np.float32), (128, 512)))
    shared["pm1"] = np.ascontiguousarray(
        np.where(np.arange(128) % 32 < 16, 1.0, -1.0).astype(np.float32)[:, None])

    maps, chunk_lists = [], []
    for c in range(8):
        b = c % 4
        chunks = CHUNKS_ROLE[c // 4]
        chunk_lists.append(chunks)
        own = np.concatenate([np.arange(ch * 512, (ch + 1) * 512) for ch in chunks])
        m = dict(shared)
        m["xf"] = _pm_tiles(np.ascontiguousarray(x[b].T))
        m["xo"] = _pm_tiles(np.ascontiguousarray(x[b][own].T))
        m["pq"] = np.ascontiguousarray(
            (own.astype(np.float32)[None, :] * invf[prow][:, None]).astype(np.float32))
        thr = np.zeros((128, 32), dtype=np.float32)
        p_idx = np.arange(128)
        for s in range(SLOTS):
            for i in range(8):
                t = EXT[s] - 8 + i
                thr[:, s * 8 + i] = (t * 128 + p_idx) - chunks[s] * 512
        m["thr"] = thr
        maps.append(m)
    return maps, chunk_lists


class Em:
    def __init__(self):
        self.ops = {e: [] for e in ENGS}
        self.cnt = {s: 0 for s in SEMS}

    def add(self, eng, fn, waits=(), inc=None, n=1, drain=False):
        self.ops[eng].append((tuple(waits), inc, n, fn, drain))
        if inc is not None:
            self.cnt[inc] += n
            return (inc, self.cnt[inc])
        return None

    def tick(self, sem):
        return (sem, self.cnt[sem])


def build_program(dumps=()):
    nc = bass.Bass("TRN2", target_bir_lowering=False)
    em = Em()

    def din(name, shape, dt=F32):
        h = nc.dram_tensor(name, shape, dt, kind="ExternalInput")
        return h, h.ap()

    xf_h, d_xf = din("xf", [128, 8 * 4096])
    xo_h, d_xo = din("xo", [128, 8 * 2048])
    _, d_wqkv = din("wqkvg", [128, NG * 8 * 768])
    _, d_wo = din("wo", [128, 8 * 1024])
    _, d_w1 = din("w1", [128, 32 * 1024])
    _, d_w2 = din("w2", [128, 32 * 1024])
    _, d_h1 = din("h1", [128, 32 * 4096])
    _, d_h2 = din("h2", [128, 32 * 4096])
    _, d_w3 = din("w3", [128, 8 * 4096])
    _, d_s1 = din("s1", [128, 32])
    _, d_s2 = din("s2", [128, 32])
    _, d_sh1 = din("sh1", [128, 32])
    _, d_sh2 = din("sh2", [128, 32])
    _, d_s3 = din("s3", [128, 8])
    _, d_lsp = din("lsp", [128, 8])
    _, d_pk = din("pk", [128, 4096])
    _, d_pq = din("pq", [128, 2048])
    _, d_qpos = din("qpos", [128, 512])
    _, d_thr = din("thr", [128, 32])
    _, d_pm1 = din("pm1", [128, 1])

    y_h = nc.dram_tensor("y", [128, 8 * 2048], F32, kind="ExternalOutput")
    d_y = y_h.ap()
    xnf_kind = "ExternalOutput" if "xn" in dumps else "Internal"
    xnf_h = nc.dram_tensor("xnf_scr", [128, 8 * 2048], F32, kind=xnf_kind)
    d_xnf = xnf_h.ap()
    _b = lambda nm, cols: nc.dram_tensor(nm, [128, cols], BF16, kind="Internal").ap()
    d_w1b = _b("w1b_scr", 32 * 1024)
    d_w2b = _b("w2b_scr", 32 * 1024)
    d_h1b = _b("h1b_scr", 32 * 4096)
    d_h2b = _b("h2b_scr", 32 * 4096)
    d_w3b = _b("w3b_scr", 8 * 4096)
    d_xfb = _b("xfb_scr", 8 * 4096)
    d_xob = _b("xob_scr", 8 * 2048)

    d_dump = {}
    for name, cols in (("att", 16384), ("kf", 8192), ("qf", 4096), ("v16", 8192)):
        if name in dumps:
            d_dump[name] = nc.dram_tensor("dump_" + name, [128, cols], BF16,
                                          kind="ExternalOutput").ap()
    for name, cols in (("ph32", PH32_COLS), ("cs", 8)):
        if name in dumps:
            d_dump[name] = nc.dram_tensor("dump_" + name, [128, cols], F32,
                                          kind="ExternalOutput").ap()

    es = contextlib.ExitStack()
    with es:
        E = es.enter_context
        ph16 = E(nc.sbuf_tensor("ph16", [128, PH16_COLS], BF16))
        ph32 = E(nc.sbuf_tensor("ph32", [128, PH32_COLS], F32))
        i32b = E(nc.sbuf_tensor("i32b", [128, 512], I32))
        c_qpos = E(nc.sbuf_tensor("c_qpos", [128, 512], F32))
        c_thr = E(nc.sbuf_tensor("c_thr", [128, 32], F32))
        c_lsp = E(nc.sbuf_tensor("c_lsp", [128, 8], F32))
        c_scale = E(nc.sbuf_tensor("c_scale", [128, 8], F32))
        c_s1 = E(nc.sbuf_tensor("c_s1", [128, 32], F32))
        c_s2 = E(nc.sbuf_tensor("c_s2", [128, 32], F32))
        c_sh1 = E(nc.sbuf_tensor("c_sh1", [128, 32], F32))
        c_sh2 = E(nc.sbuf_tensor("c_sh2", [128, 32], F32))
        c_s3 = E(nc.sbuf_tensor("c_s3", [128, 8], F32))
        c_misc = E(nc.sbuf_tensor("c_misc", [128, 8], F32))
        c_ones1 = E(nc.sbuf_tensor("c_ones1", [1, 128], F32))
        c_onesb = E(nc.sbuf_tensor("c_onesb", [128, 1], BF16))
        c_exp = E(nc.sbuf_tensor("c_exp", [128, 8], F32))

        P = [E(nc.psum_tensor(f"P{i}", [128, 512], F32)) for i in range(6)]
        S0 = E(nc.psum_tensor("S0", [1, 512], F32))
        S1 = E(nc.psum_tensor("S1", [1, 512], F32))

        sems = {k: E(nc.semaphore("sem_" + k)) for k in SEMS}

        def t16(off, ln):
            return ph16[:, off:off + ln]

        def t32(off, ln):
            return ph32[:, off:off + ln]

        ds, dg, mt, ms, mv = SEMS

        for dst, src in ((c_qpos, d_qpos), (c_thr, d_thr), (c_lsp, d_lsp),
                         (c_s1, d_s1), (c_s2, d_s2), (c_sh1, d_sh1),
                         (c_sh2, d_sh2), (c_s3, d_s3)):
            em.add("y", lambda e, dst=dst, src=src: e.dma_start(dst[:], src),
                   inc=ds, n=16)
        em.add("y", lambda e: e.dma_start(c_misc[:, 0:1], d_pm1), inc=ds, n=16)
        t_setup = em.tick(ds)

        em.add("v", lambda e: e.memset(c_misc[:, 1:2], EXP_SHIFT))
        em.add("v", lambda e: e.memset(c_misc[:, 2:3], 1e-5))
        em.add("v", lambda e: e.memset(c_misc[:, 3:4], math.pi / 2))
        em.add("v", lambda e: e.memset(c_misc[:, 4:5], 1.0))
        em.add("v", lambda e: e.memset(c_ones1[:], 1.0))
        em.add("v", lambda e: e.memset(c_onesb[:], 1.0))
        em.add("v", lambda e: e.memset(c_exp[:], 1.0), waits=[t_setup])
        for n_ in range(12, 0, -1):
            em.add("v", lambda e, n_=n_: e.tensor_scalar(
                c_scale[:], c_lsp[:], 1.0 / n_, 0.0, ALU.mult, ALU.add), drain=True)
            em.add("v", lambda e: e.tensor_tensor(c_exp[:], c_scale[:], c_exp[:], ALU.mult),
                   drain=True)
            em.add("v", lambda e: e.tensor_scalar(c_exp[:], c_exp[:], 1.0, 1.0, ALU.mult,
                                                  ALU.add), drain=True)
        t_scale = em.add("v", lambda e: e.tensor_copy(c_scale[:], c_exp[:]), inc=mv,
                         drain=True)

        pm1 = c_misc[:, 0:1]
        neg32 = c_misc[:, 1:2]
        epsv = c_misc[:, 2:3]
        hpiv = c_misc[:, 3:4]
        ones32 = c_misc[:, 4:5]

        kf = lambda pp: O_KF + pp * 4096
        qf = lambda pp: O_QF + pp * 2048

        def rope_tables(hp_global, src_sb, w_dma):
            th = t32(A_ROPE, 512)
            cosb = t32(A_ROPE + 512, 512)
            sinb = t32(A_ROPE + 1024, 512)
            nf = t32(A_ROPE + 1536, 512)
            em.add("s", lambda e: e.activation(
                th, src_sb, AF.Copy, scale=c_scale[:, hp_global:hp_global + 1]),
                waits=[t_scale, w_dma, em.tick(mv)], drain=True)
            em.add("s", lambda e: e.activation(i32b[:], th, AF.Copy, scale=1.0 / TWO_PI))
            tk_n = em.add("s", lambda e: e.copy(nf, i32b[:]), inc=ms)
            em.add("v", lambda e: e.scalar_tensor_tensor(th, nf, -6.28125, th,
                                                         ALU.mult, ALU.add), waits=[tk_n])
            em.add("v", lambda e: e.scalar_tensor_tensor(th, nf, -CW3, th, ALU.mult, ALU.add))
            tk_r = em.add("v", lambda e: e.tensor_scalar(
                cosb, th, math.pi / 2, 0.0, ALU.is_gt, ALU.add), inc=mv)
            em.add("s", lambda e: e.activation(sinb, th, AF.Sin, scale=pm1), waits=[tk_r])
            tk_rc = em.add("v", lambda e: e.scalar_tensor_tensor(
                cosb, cosb, -TWO_PI, th, ALU.mult, ALU.add), inc=mv)
            tk_cos = em.add("s", lambda e: e.activation(cosb, cosb, AF.Sin, bias=hpiv),
                            waits=[tk_rc], inc=ms)
            return cosb, sinb, tk_cos

        SWAP16 = list(range(16, 32)) + list(range(0, 16))

        def rope_apply(psrc, cosb, sinb, dst16, tk_mm, tk_tab):
            swp = t32(A_ROPE + 2048, 512)
            em.add("v", lambda e: e.tensor_tensor(swp, psrc[:], sinb, ALU.mult),
                   waits=[tk_mm, tk_tab])
            em.add("v", lambda e: e.stream_shuffle(swp, swp, SWAP16), drain=True)
            em.add("v", lambda e: e.tensor_tensor(cosb, psrc[:], cosb, ALU.mult))
            return em.add("v", lambda e: e.tensor_tensor(dst16, cosb, swp, ALU.add),
                          inc=mv, drain=True)

        NKT = 8

        tk_vones = em.add("v", lambda e: e.memset(
            bass.AP(ph16, O_V16 + 64, [[PH16_COLS, 128], [68, 128]]), 1.0), inc=mv)

        rope_prev_apply = [None, None]
        rope_ip = [0]

        def rope_tables(hp_global, src_sb, w_dma):
            ip = rope_ip[0] % 2
            rope_ip[0] += 1
            base = A_ROPE + ip * 2048
            th = t32(base, 512)
            cosb = t32(base + 512, 512)
            sinb = t32(base + 1024, 512)
            nf = t32(base + 1536, 512)
            w0 = [t_scale, w_dma]
            if rope_prev_apply[ip] is not None:
                w0.append(rope_prev_apply[ip])
            em.add("s", lambda e: e.activation(
                th, src_sb, AF.Copy, scale=c_scale[:, hp_global:hp_global + 1]),
                waits=w0)
            em.add("s", lambda e: e.activation(i32b[:], th, AF.Copy, scale=1.0 / TWO_PI))
            tk_n = em.add("s", lambda e: e.copy(nf, i32b[:]), inc=ms)
            em.add("v", lambda e: e.scalar_tensor_tensor(th, nf, -6.28125, th,
                                                         ALU.mult, ALU.add), waits=[tk_n])
            em.add("v", lambda e: e.scalar_tensor_tensor(th, nf, -CW3, th, ALU.mult, ALU.add))
            tk_r = em.add("v", lambda e: e.tensor_scalar(
                cosb, th, math.pi / 2, 0.0, ALU.is_gt, ALU.add), inc=mv)
            em.add("s", lambda e: e.activation(sinb, th, AF.Sin, scale=pm1), waits=[tk_r])
            tk_rc = em.add("v", lambda e: e.scalar_tensor_tensor(
                cosb, cosb, -TWO_PI, th, ALU.mult, ALU.add), inc=mv)
            tk_cos = em.add("s", lambda e: e.activation(cosb, cosb, AF.Sin, bias=hpiv),
                            waits=[tk_rc], inc=ms)
            return ip, cosb, sinb, tk_cos

        SWAP16 = list(range(16, 32)) + list(range(0, 16))

        def rope_apply(ip, psrc, cosb, sinb, dst16, tk_mm, tk_tab):
            swp = t32(A_ROPE + 4096, 512)
            em.add("v", lambda e: e.tensor_tensor(swp, psrc[:], sinb, ALU.mult),
                   waits=[tk_mm, tk_tab])
            em.add("v", lambda e: e.stream_shuffle(swp, swp, SWAP16), drain=True)
            em.add("v", lambda e: e.tensor_tensor(cosb, psrc[:], cosb, ALU.mult))
            tk = em.add("v", lambda e: e.tensor_tensor(dst16, cosb, swp, ALU.add),
                        inc=mv, drain=True)
            rope_prev_apply[ip] = tk
            return tk

        NKT = 8
        for g in range(NG):
            wq16 = t16(O_WQ, 8 * 768)
            tk_wq = em.add("g", lambda e, g=g: e.dma_start(
                wq16, d_wqkv[:, g * 8 * 768:(g + 1) * 8 * 768]),
                waits=[em.tick(mt)], inc=dg, n=16)

            xc_guard = {}
            kq_psum_guard = [None, None]
            vevac_tick = {}
            for tc in range(8):
                xc = t16(O_XC + (tc % 2) * 4096, 4096)
                w = list(xc_guard[tc - 2]) if tc >= 2 else [em.tick(mt)]
                if g == 0:
                    tk_x = em.add("g", lambda e, tc=tc, xc=xc: e.dma_start(
                        xc, bass.AP(xf_h, tc * 512, [[8 * 4096, 128], [4096, 8], [1, 512]])),
                        waits=w, inc=dg, n=16)
                    em.add("y", lambda e, tc=tc, xc=xc: e.dma_start(
                        d_xfb[:, tc * 4096:(tc + 1) * 4096], xc), waits=[tk_x], inc=ds, n=16)
                else:
                    tk_x = em.add("y", lambda e, tc=tc, xc=xc: e.dma_start(
                        xc, d_xfb[:, tc * 4096:(tc + 1) * 4096]), waits=w, inc=ds, n=16)
                pk_sb = t32(A_PK + (tc % 2) * 512, 512)
                tk_pk = em.add("y", lambda e, tc=tc, pk_sb=pk_sb: e.dma_start(
                    pk_sb, d_pk[:, tc * 512:(tc + 1) * 512]),
                    waits=[em.tick(ms)], inc=ds, n=16)
                for pp in range(2):
                    hp_global = 2 * g + pp
                    ps_k = P[pp]
                    wk = [tk_x, tk_wq]
                    if kq_psum_guard[pp] is not None:
                        wk.append(kq_psum_guard[pp])
                    else:
                        wk.append(em.tick(ms))
                        wk.append(em.tick(mv))
                    def kmm(e, pp=pp, xc=xc, ps_k=ps_k):
                        for kt in range(NKT):
                            mm = e.matmul(
                                ps_k[:],
                                wq16[:, kt * 768 + 256 + pp * 128:kt * 768 + 256 + (pp + 1) * 128],
                                xc[:, kt * 512:(kt + 1) * 512],
                                start=(kt == 0), stop=(kt == NKT - 1))
                        return mm
                    tkm = em.add("t", kmm, waits=wk, inc=mt)
                    ip, cosb, sinb, tk_tab = rope_tables(hp_global, pk_sb, tk_pk)
                    kq_psum_guard[pp] = rope_apply(
                        ip, ps_k, cosb, sinb, t16(kf(pp) + tc * 512, 512), tkm, tk_tab)
                for tt in range(4):
                    ps_v = P[2 + tt % 2]
                    tt_global = tc * 4 + tt
                    wv = [tk_x, tk_wq]
                    prev = vevac_tick.get(tt_global - 2)
                    if prev is not None:
                        wv.append(prev)
                    else:
                        wv.append(em.tick(ms))
                        wv.append(em.tick(mv))
                    def vmm(e, xc=xc, tt=tt, ps_v=ps_v):
                        for kt in range(NKT):
                            mm = e.matmul(
                                ps_v[:, 0:256],
                                xc[:, kt * 512 + tt * 128:kt * 512 + (tt + 1) * 128],
                                wq16[:, kt * 768 + 512:kt * 768 + 768],
                                start=(kt == 0), stop=(kt == NKT - 1))
                        return mm
                    tkv = em.add("t", vmm, waits=wv, inc=mt)
                    def vevac(e, ps_v=ps_v, tt_global=tt_global):
                        for hg in range(4):
                            ins = e.tensor_copy(t16(O_V16 + tt_global * 272 + hg * 68, 64),
                                                ps_v[:, hg * 64:(hg + 1) * 64])
                        return ins
                    vevac_tick[tt_global] = em.add("v", vevac, waits=[tkv], inc=mv)
                xc_guard[tc] = (em.tick(mt),) if g > 0 else (em.tick(mt), em.tick(ds))
            for qc in range(4):
                xc = t16(O_XC + (qc % 2) * 4096, 4096)
                if qc < 2:
                    w = list(xc_guard[6 + qc])
                else:
                    w = [em.tick(mt)] if g > 0 else [em.tick(mt), em.tick(ds)]
                if g == 0:
                    tk_x = em.add("g", lambda e, qc=qc, xc=xc: e.dma_start(
                        xc, bass.AP(xo_h, qc * 512, [[8 * 2048, 128], [2048, 8], [1, 512]])),
                        waits=w, inc=dg, n=16)
                    em.add("y", lambda e, qc=qc, xc=xc: e.dma_start(
                        d_xob[:, qc * 4096:(qc + 1) * 4096], xc), waits=[tk_x], inc=ds, n=16)
                else:
                    tk_x = em.add("y", lambda e, qc=qc, xc=xc: e.dma_start(
                        xc, d_xob[:, qc * 4096:(qc + 1) * 4096]), waits=w, inc=ds, n=16)
                pq_sb = t32(A_PK + (qc % 2) * 512, 512)
                tk_pq = em.add("y", lambda e, qc=qc, pq_sb=pq_sb: e.dma_start(
                    pq_sb, d_pq[:, qc * 512:(qc + 1) * 512]),
                    waits=[em.tick(ms)], inc=ds, n=16)
                for pp in range(2):
                    hp_global = 2 * g + pp
                    ps_q = P[pp]
                    def qmm(e, pp=pp, xc=xc, ps_q=ps_q):
                        for kt in range(NKT):
                            mm = e.matmul(
                                ps_q[:],
                                wq16[:, kt * 768 + pp * 128:kt * 768 + (pp + 1) * 128],
                                xc[:, kt * 512:(kt + 1) * 512],
                                start=(kt == 0), stop=(kt == NKT - 1))
                        return mm
                    tkm = em.add("t", qmm, waits=[tk_x, kq_psum_guard[pp]], inc=mt)
                    ip, cosb, sinb, tk_tab = rope_tables(hp_global, pq_sb, tk_pq)
                    kq_psum_guard[pp] = rope_apply(
                        ip, ps_q, cosb, sinb, t16(qf(pp) + qc * 512, 512), tkm, tk_tab)

            fin_prev = {"tkc": None, "norm": None, "exp_last": None}
            for s in range(SLOTS):
                ext = EXT[s]
                for pp in range(2):
                    exp_tick = {}
                    av_tick = {}
                    sc_tick = {}

                    def emit_sc(t, pp=pp, s=s, ext=ext):
                        pb = t % 2
                        psA = [P[0], P[4]][pb]
                        psB = [P[1], P[5]][pb]
                        def smm(e, pp=pp, s=s, t=t, psA=psA, psB=psB):
                            e.matmul(psA[:], t16(kf(pp) + t * 128, 128)[0:64, :],
                                     t16(qf(pp) + s * 512, 512)[0:64, :],
                                     start=True, stop=True)
                            return e.matmul(psB[:], t16(kf(pp) + t * 128, 128)[64:128, :],
                                            t16(qf(pp) + s * 512, 512)[64:128, :],
                                            start=True, stop=True)
                        if t >= 2:
                            w = [exp_tick[t - 2]]
                        elif fin_prev["tkc"] is not None:
                            w = [fin_prev["tkc"], fin_prev["exp_last"]]
                        else:
                            w = [em.tick(ms), em.tick(mv)]
                        sc_tick[t] = em.add("t", smm, waits=w, inc=mt)

                    emit_sc(0)
                    if ext > 1:
                        emit_sc(1)
                    for t in range(ext):
                        pb = t % 2
                        psA = [P[0], P[4]][pb]
                        psB = [P[1], P[5]][pb]
                        expA = t16(O_EXP + pb * 512, 512)
                        expB = t16(O_EXP + 1024 + pb * 512, 512)
                        wE = [sc_tick[t]]
                        if t >= 2:
                            wE.append(av_tick[t - 2])
                        tkeA = em.add("s", lambda e, psA=psA, expA=expA: e.activation(
                            expA, psA[:], AF.Exp, bias=neg32, scale=0.125),
                            waits=wE, inc=ms)
                        tkeB = em.add("s", lambda e, psB=psB, expB=expB: e.activation(
                            expB, psB[:], AF.Exp, bias=neg32, scale=0.125), inc=ms)
                        exp_tick[t] = tkeB
                        wavA, wavB = [tkeA], [tkeB]
                        if t >= ext - 8:
                            col = s * 8 + (t - (ext - 8))
                            tkMA = em.add("v", lambda e, expA=expA, col=col: e.scalar_tensor_tensor(
                                expA, c_qpos[:], c_thr[:, col:col + 1], expA,
                                ALU.is_ge, ALU.mult), waits=[tkeA], inc=mv)
                            tkMB = em.add("v", lambda e, expB=expB, col=col: e.scalar_tensor_tensor(
                                expB, c_qpos[:], c_thr[:, col:col + 1], expB,
                                ALU.is_ge, ALU.mult), waits=[tkeB], inc=mv)
                            wavA, wavB = [tkMA], [tkMB]
                        if t == 0:
                            wavA.append(fin_prev["norm"] if fin_prev["norm"] is not None
                                        else em.tick(mv))
                            wavA.append(tk_vones)
                        if t + 2 < ext:
                            emit_sc(t + 2)
                        tka = em.add("t", lambda e, t=t, pp=pp, expA=expA, ext=ext: e.matmul(
                            P[2][0:65, :], t16(O_V16 + t * 272 + (2 * pp) * 68, 65),
                            expA, start=(t == 0), stop=(t == ext - 1)),
                            waits=wavA, inc=mt)
                        av_tick[t] = em.add("t", lambda e, t=t, pp=pp, expB=expB, ext=ext: e.matmul(
                            P[3][0:65, :], t16(O_V16 + t * 272 + (2 * pp + 1) * 68, 65),
                            expB, start=(t == 0), stop=(t == ext - 1)),
                            waits=wavB, inc=mt)
                    rbA = t32(W_R, 512)
                    rbB = t32(W_R + 1024, 512)
                    nt = t32(W_R + 2048, 512)
                    tk_last = av_tick[ext - 1]
                    em.add("s", lambda e, rbA=rbA: e.activation(
                        rbA[0:1, :], P[2][64:65, :], AF.Ln), waits=[tk_last], drain=True)
                    em.add("s", lambda e, rbA=rbA: e.activation(
                        rbA[0:1, :], rbA[0:1, :], AF.Exp, scale=-1.0))
                    em.add("s", lambda e, rbB=rbB: e.activation(
                        rbB[0:1, :], P[3][64:65, :], AF.Ln))
                    tk_sd = em.add("s", lambda e, rbB=rbB: e.activation(
                        rbB[0:1, :], rbB[0:1, :], AF.Exp, scale=-1.0), inc=ms)
                    em.add("v", lambda e, rbA=rbA, nt=nt: e.tensor_tensor(
                        nt[0:1, :], P[2][64:65, :], rbA[0:1, :], ALU.mult), waits=[tk_sd])
                    em.add("v", lambda e, nt=nt: e.tensor_scalar(
                        nt[0:1, :], nt[0:1, :], -1.0, 2.0, ALU.mult, ALU.add))
                    em.add("v", lambda e, rbA=rbA, nt=nt: e.tensor_tensor(
                        rbA[0:1, :], rbA[0:1, :], nt[0:1, :], ALU.mult))
                    em.add("v", lambda e, rbB=rbB, nt=nt: e.tensor_tensor(
                        nt[0:1, :], P[3][64:65, :], rbB[0:1, :], ALU.mult))
                    em.add("v", lambda e, nt=nt: e.tensor_scalar(
                        nt[0:1, :], nt[0:1, :], -1.0, 2.0, ALU.mult, ALU.add))
                    tkrec = em.add("v", lambda e, rbB=rbB, nt=nt: e.tensor_tensor(
                        rbB[0:1, :], rbB[0:1, :], nt[0:1, :], ALU.mult), inc=mv)
                    def bcast(e, rbA=rbA, rbB=rbB):
                        e.matmul(P[0][0:64, :], c_ones1[:, 0:64], rbA[0:1, :],
                                 start=True, stop=True, tile_position=(0, 0))
                        return e.matmul(P[0][64:128, :], c_ones1[:, 0:64], rbB[0:1, :],
                                        start=True, stop=True, tile_position=(0, 64))
                    tkb = em.add("t", bcast, waits=[tkrec, em.tick(ms)], inc=mt)
                    tkc = em.add("s", lambda e, rbA=rbA: e.copy(rbA, P[0][:]),
                                 waits=[tkb], inc=ms)
                    pair_idx = 2 * g + pp
                    em.add("v", lambda e, pair_idx=pair_idx, s=s, rbA=rbA: e.tensor_tensor(
                        t16(O_ATT + pair_idx * 2048 + s * 512, 512)[0:64, :],
                        P[2][0:64, :], rbA[0:64, :], ALU.mult), waits=[tkc])
                    tk_norm = em.add("v", lambda e, pair_idx=pair_idx, s=s, rbA=rbA:
                                     e.tensor_tensor(
                                         t16(O_ATT + pair_idx * 2048 + s * 512, 512)[64:128, :],
                                         P[3][0:64, :], rbA[64:128, :], ALU.mult), inc=mv)
                    fin_prev = {"tkc": tkc, "norm": tk_norm, "exp_last": exp_tick[ext - 1]}

        t_att = em.tick(mv)
        for nm, off, cols in (("att", O_ATT, 16384), ("kf", O_KF, 8192),
                              ("qf", O_QF, 4096), ("v16", O_V16, 8192)):
            if nm in d_dump:
                em.add("y", lambda e, nm=nm, off=off, cols=cols: e.dma_start(
                    d_dump[nm], t16(off, cols)), waits=[t_att], inc=ds, n=16)
        if "ph32" in d_dump:
            em.add("y", lambda e: e.dma_start(d_dump["ph32"], ph32[:]),
                   waits=[t_att], inc=ds, n=16)
        if "cs" in d_dump:
            em.add("y", lambda e: e.dma_start(d_dump["cs"], c_scale[:]),
                   waits=[t_att], inc=ds, n=16)

        wo16 = t16(O_WO, 8192)
        tk_wo = em.add("g", lambda e: e.dma_start(wo16, d_wo),
                       waits=[em.tick(mt)], inc=dg, n=16)
        r_sb = t32(W_R, 4096)
        for qc in range(4):
            xo_sb = t32(W_XO, 4096)
            w_prev = [em.tick(ds)] if qc == 0 else [tk_spill]
            tk_xo = em.add("y", lambda e, qc=qc: e.dma_start(
                xo_sb, bass.AP(xo_h, qc * 512, [[8 * 2048, 128], [2048, 8], [1, 512]])),
                waits=[em.tick(mv)], inc=ds, n=16)
            ev = []
            for dt in range(8):
                psy = P[dt % 2]
                def womm(e, dt=dt, qc=qc, psy=psy):
                    for kt in range(8):
                        mm = e.matmul(psy[:],
                                      wo16[:, kt * 1024 + dt * 128:kt * 1024 + (dt + 1) * 128],
                                      t16(O_ATT + kt * 2048 + qc * 512, 512),
                                      start=(kt == 0), stop=(kt == 7))
                    return mm
                w = [tk_wo, t_att]
                if dt >= 2:
                    w.append(ev[dt - 2])
                tkm = em.add("t", womm, waits=w, inc=mt)
                tke = em.add("v", lambda e, dt=dt, psy=psy: e.tensor_tensor(
                    r_sb[:, dt * 512:(dt + 1) * 512], psy[:],
                    xo_sb[:, dt * 512:(dt + 1) * 512], ALU.add),
                    waits=[tkm, tk_xo], inc=mv)
                ev.append(tke)
            r16 = t16(F_H, 4096)
            rq16 = t16(F_H + 4096, 4096)
            cps = []
            for dt in range(8):
                em.add("s", lambda e, dt=dt: e.copy(
                    r16[:, dt * 512:(dt + 1) * 512], r_sb[:, dt * 512:(dt + 1) * 512]),
                    waits=[ev[dt]])
                tkc_ = em.add("s", lambda e, dt=dt: e.activation(
                    rq16[:, dt * 512:(dt + 1) * 512], r_sb[:, dt * 512:(dt + 1) * 512],
                    AF.Square), inc=ms)
                cps.append(tkc_)
            def lnsum(e):
                for dt in range(8):
                    e.matmul(S0[:], c_onesb[:], r16[:, dt * 512:(dt + 1) * 512],
                             start=(dt == 0), stop=(dt == 7))
                for dt in range(8):
                    mm = e.matmul(S1[:], c_onesb[:], rq16[:, dt * 512:(dt + 1) * 512],
                                  start=(dt == 0), stop=(dt == 7))
                return mm
            tk_sum = em.add("t", lnsum, waits=[cps[-1]], inc=mt)
            sq = [tk_sum]
            lnm = t32(LN_M, 512)[0:1, :]
            lnv = t32(LN_V, 512)[0:1, :]
            lnt = t32(LN_T, 512)[0:1, :]
            lns = t32(LN_S, 512)[0:1, :]
            rstd_t = t32(LN_RSTD, 512)[0:1, :]
            em.add("v", lambda e: e.tensor_scalar(lnm, S0[:], 1.0 / 1024, 0.0,
                                                  ALU.mult, ALU.add), waits=[tk_sum, sq[-1]])
            em.add("v", lambda e: e.tensor_tensor(lnt, lnm, lnm, ALU.mult), drain=True)
            em.add("v", lambda e: e.tensor_scalar(lnv, S1[:], 1.0 / 1024, 0.0, ALU.mult, ALU.add))
            tkv_ = em.add("v", lambda e: e.tensor_tensor(lnv, lnv, lnt,
                                                         ALU.subtract), inc=mv, drain=True)
            em.add("s", lambda e: e.activation(lns, lnv, AF.Ln,
                                               bias=epsv[0:1, :]),
                   waits=[tkv_], drain=True)
            tkr = em.add("s", lambda e: e.activation(rstd_t, lns, AF.Exp, scale=-0.5),
                         drain=True, inc=ms)
            def lnb(e):
                e.matmul(P[2][:], c_ones1[:], lnm, start=True, stop=True)
                return e.matmul(P[3][:], c_ones1[:], rstd_t, start=True, stop=True)
            tkb = em.add("t", lnb, waits=[tkr], inc=mt)
            for dt in range(8):
                em.add("v", lambda e, dt=dt: e.tensor_tensor(
                    xo_sb[:, dt * 512:(dt + 1) * 512], r_sb[:, dt * 512:(dt + 1) * 512],
                    P[2][:], ALU.subtract), waits=[tkb])
                tka = em.add("v", lambda e, dt=dt: e.tensor_tensor(
                    xo_sb[:, dt * 512:(dt + 1) * 512], xo_sb[:, dt * 512:(dt + 1) * 512],
                    P[3][:], ALU.mult), inc=mv)
            tk_spill = em.add("y", lambda e, qc=qc: e.dma_start(
                bass.AP(xnf_h, qc * 512, [[8 * 2048, 128], [2048, 8], [1, 512]]),
                xo_sb), waits=[tka], inc=ds, n=16)
        t_w = em.tick(ds)

        prev_f1_mt = None
        prev_f3_mt = None
        for ci, CH in enumerate(FCH):
            coff = FCH_OFF[ci]
            xn16 = t16(F_XN, 8 * CH)
            wxn = [t_w] if prev_f1_mt is None else [t_w, prev_f1_mt]
            tk_xn = em.add("g", lambda e, coff=coff, CH=CH, xn16=xn16: e.dma_start(
                xn16, bass.AP(xnf_h, coff, [[8 * 2048, 128], [2048, 8], [1, CH]])),
                waits=wxn, inc=dg, n=16)

            f1_guard = {}
            h_tick = {}
            f1_evac = {}
            for ot in range(32):
                pbuf = ot % 2
                w1p = t16(F_W1P + pbuf * 1024, 1024)
                w2p = t16(F_W2P + pbuf * 1024, 1024)
                if ci == 0:
                    wdep = [f1_guard[ot - 2]] if ot >= 2 else []
                else:
                    wdep = [f1_guard[ot - 2]] if ot >= 2 else [prev_f1_mt]
                if ci == 0:
                    tk1 = em.add("g", lambda e, ot=ot, w1p=w1p: e.dma_start(
                        w1p, d_w1[:, ot * 1024:(ot + 1) * 1024]), waits=wdep, inc=dg, n=16)
                    tkw = em.add("g", lambda e, ot=ot, w2p=w2p: e.dma_start(
                        w2p, d_w2[:, ot * 1024:(ot + 1) * 1024]), inc=dg, n=16)
                    em.add("y", lambda e, ot=ot, w1p=w1p: e.dma_start(
                        d_w1b[:, ot * 1024:(ot + 1) * 1024], w1p), waits=[tk1], inc=ds, n=16)
                    em.add("y", lambda e, ot=ot, w2p=w2p: e.dma_start(
                        d_w2b[:, ot * 1024:(ot + 1) * 1024], w2p), waits=[tkw], inc=ds, n=16)
                else:
                    tk1 = em.add("y", lambda e, ot=ot, w1p=w1p: e.dma_start(
                        w1p, d_w1b[:, ot * 1024:(ot + 1) * 1024]), waits=wdep, inc=ds, n=16)
                    tkw = em.add("y", lambda e, ot=ot, w2p=w2p: e.dma_start(
                        w2p, d_w2b[:, ot * 1024:(ot + 1) * 1024]), inc=ds, n=16)
                nsoff = 0
                for nsi, ns in enumerate(NSUB[CH]):
                    par = (ot * 2 + nsi) % 2
                    sub_idx = ot * len(NSUB[CH]) + nsi
                    psa = P[0 + par]
                    psb = P[2 + par]
                    ub = t32(A_UBWB + par * 1024, 512)
                    wb = t32(A_UBWB + par * 1024 + 512, 512)
                    def f1mm(e, w1p=w1p, w2p=w2p, xn16=xn16, nsoff=nsoff, ns=ns,
                             psa=psa, psb=psb, CH=CH):
                        for kt in range(8):
                            e.matmul(psa[:, 0:ns], w1p[:, kt * 128:(kt + 1) * 128],
                                     xn16[:, kt * CH + nsoff:kt * CH + nsoff + ns],
                                     start=(kt == 0), stop=(kt == 7))
                        for kt in range(8):
                            mm = e.matmul(psb[:, 0:ns], w2p[:, kt * 128:(kt + 1) * 128],
                                          xn16[:, kt * CH + nsoff:kt * CH + nsoff + ns],
                                          start=(kt == 0), stop=(kt == 7))
                        return mm
                    wmm1 = [tk1, tkw, tk_xn]
                    if f1_evac.get(sub_idx - 2) is not None:
                        wmm1.append(f1_evac[sub_idx - 2])
                    else:
                        wmm1.append(em.tick(mv))
                    tkm = em.add("t", f1mm, waits=wmm1, inc=mt)
                    wS = [tkm]
                    hist = h_tick.get((ot * 2 + nsi) - 2)
                    if hist is not None:
                        wS.append(hist)
                    em.add("s", lambda e, psa=psa, ot=ot, ns=ns, ub=ub: e.activation(
                        ub[:, 0:ns], psa[:, 0:ns], AF.Silu, scale=c_s1[:, ot:ot + 1]),
                        waits=wS)
                    tke = em.add("s", lambda e, psb=psb, ot=ot, ns=ns, wb=wb: e.activation(
                        wb[:, 0:ns], psb[:, 0:ns], AF.Copy, scale=c_s2[:, ot:ot + 1]),
                        inc=ms)
                    f1_evac[sub_idx] = tke
                    tkh = em.add("v", lambda e, ot=ot, ns=ns, nsoff=nsoff, CH=CH, ub=ub,
                                 wb=wb: e.tensor_tensor(
                                     t16(F_H + ot * CH + nsoff, ns), ub[:, 0:ns],
                                     wb[:, 0:ns], ALU.mult), waits=[tke], inc=mv)
                    h_tick[ot * 2 + nsi] = tkh
                    nsoff += ns
                f1_guard[ot] = em.tick(mv)
            prev_f1_mt = em.tick(mt)

            for phase_i, (dsrc, dbsc, src_off, dst_off, ssc) in enumerate(
                    ((d_h1, d_h1b, F_H, F_C, c_sh1), (d_h2, d_h2b, F_C, F_H, c_sh2))):
                pg = {}
                t_src_ready = em.tick(mv) if phase_i == 0 else em.tick(ms)
                for ot in range(32):
                    pbuf = ot % 2
                    hp = t16(F_HP + pbuf * 4096, 4096)
                    if ci == 0 and phase_i == 0:
                        wdep = [pg[ot - 2]] if ot >= 2 else []
                    elif phase_i == 0 and prev_f3_mt is not None:
                        wdep = [pg[ot - 2]] if ot >= 2 else [prev_f3_mt]
                    else:
                        wdep = [pg[ot - 2]] if ot >= 2 else [em.tick(mt), em.tick(mv)]
                    if ci == 0:
                        tkp = em.add("g", lambda e, ot=ot, hp=hp, dsrc=dsrc: e.dma_start(
                            hp, dsrc[:, ot * 4096:(ot + 1) * 4096]), waits=wdep,
                            inc=dg, n=16)
                        em.add("y", lambda e, ot=ot, hp=hp, dbsc=dbsc: e.dma_start(
                            dbsc[:, ot * 4096:(ot + 1) * 4096], hp), waits=[tkp],
                            inc=ds, n=16)
                    else:
                        tkp = em.add("y", lambda e, ot=ot, hp=hp, dbsc=dbsc: e.dma_start(
                            hp, dbsc[:, ot * 4096:(ot + 1) * 4096]), waits=wdep,
                            inc=ds, n=16)
                    nsoff = 0
                    prev_evac = None
                    for ns in NSUB[CH]:
                        psc_ = P[0 + (ot % 2)]
                        wmm = [tkp, t_src_ready]
                        if ot >= 2:
                            wmm.append(pg[ot - 2])
                        if prev_evac is not None:
                            wmm.append(prev_evac)
                        def f2mm(e, hp=hp, src_off=src_off, nsoff=nsoff, ns=ns,
                                 psc_=psc_, CH=CH):
                            for kt in range(32):
                                mm = e.matmul(psc_[:, 0:ns], hp[:, kt * 128:(kt + 1) * 128],
                                              t16(src_off + kt * CH + nsoff, ns),
                                              start=(kt == 0), stop=(kt == 31))
                            return mm
                        tkm = em.add("t", f2mm, waits=wmm, inc=mt)
                        prev_evac = em.add("s", lambda e, psc_=psc_, ot=ot, ns=ns,
                                           dst_off=dst_off, nsoff=nsoff, CH=CH, ssc=ssc:
                                           e.activation(
                                               t16(dst_off + ot * CH + nsoff, ns),
                                               psc_[:, 0:ns],
                                               AF.Gelu, scale=ssc[:, ot:ot + 1]),
                                           waits=[tkm], inc=ms)
                        nsoff += ns
                    pg[ot] = em.tick(ms)
                if phase_i == 1:
                    prev_f3_mt = em.tick(mt)

            nsoff = 0
            t_d_ready = None
            for nsi, ns in enumerate(NSUB[CH]):
                if t_d_ready is None:
                    t_d_ready = em.tick(ms)
                ev4 = {}
                for ot in range(8):
                    pbuf = ot % 2
                    w3p = t16(F_W3P + pbuf * 4096, 4096)
                    wdep = [ev4[ot - 2]] if ot >= 2 else [em.tick(mt), em.tick(mv)]
                    if ci == 0 and nsi == 0:
                        tkp = em.add("g", lambda e, ot=ot, w3p=w3p: e.dma_start(
                            w3p, d_w3[:, ot * 4096:(ot + 1) * 4096]), waits=wdep,
                            inc=dg, n=16)
                        em.add("y", lambda e, ot=ot, w3p=w3p: e.dma_start(
                            d_w3b[:, ot * 4096:(ot + 1) * 4096], w3p), waits=[tkp],
                            inc=ds, n=16)
                    else:
                        tkp = em.add("y", lambda e, ot=ot, w3p=w3p: e.dma_start(
                            w3p, d_w3b[:, ot * 4096:(ot + 1) * 4096]), waits=wdep,
                            inc=ds, n=16)
                    xt_ = t32(W_XNF + (ot % 2) * 512, 512)
                    wx = [ev4[ot - 2]] if ot >= 2 else []
                    tk_x2 = em.add("y", lambda e, coff=coff, nsoff=nsoff, ot=ot, ns=ns,
                                   xt_=xt_: e.dma_start(
                                       xt_[:, 0:ns],
                                       d_xnf[:, ot * 2048 + coff + nsoff:
                                             ot * 2048 + coff + nsoff + ns]),
                                   waits=wx, inc=ds, n=16)
                    psy = P[4 + (ot % 2)]
                    def f4mm(e, w3p=w3p, nsoff=nsoff, ns=ns, psy=psy, CH=CH):
                        for kt in range(32):
                            mm = e.matmul(psy[:, 0:ns], w3p[:, kt * 128:(kt + 1) * 128],
                                          t16(F_H + kt * CH + nsoff, ns),
                                          start=(kt == 0), stop=(kt == 31))
                        return mm
                    wmm4 = [tkp, t_d_ready]
                    if ot >= 2:
                        wmm4.append(ev4[ot - 2])
                    else:
                        wmm4.append(em.tick(mv))
                    tkm = em.add("t", f4mm, waits=wmm4, inc=mt)
                    tke = em.add("v", lambda e, psy=psy, ot=ot, ns=ns, xt_=xt_:
                                 e.scalar_tensor_tensor(
                                     r_sb[:, ot * 512:ot * 512 + ns], psy[:, 0:ns],
                                     c_s3[:, ot:ot + 1], xt_[:, 0:ns],
                                     ALU.mult, ALU.add), waits=[tkm, tk_x2], inc=mv)
                    ev4[ot] = tke
                r16 = t16(F_C, 4096)
                rq16 = t16(F_C + 4096, 4096)
                cps = []
                for dt in range(8):
                    em.add("s", lambda e, dt=dt, ns=ns: e.copy(
                        r16[:, dt * 512:dt * 512 + ns], r_sb[:, dt * 512:dt * 512 + ns]),
                        waits=[ev4[dt]])
                    tkc_ = em.add("s", lambda e, dt=dt, ns=ns: e.activation(
                        rq16[:, dt * 512:dt * 512 + ns], r_sb[:, dt * 512:dt * 512 + ns],
                        AF.Square), inc=ms)
                    cps.append(tkc_)
                def ln2sum(e, ns=ns):
                    for dt in range(8):
                        e.matmul(S0[:, 0:ns], c_onesb[:], r16[:, dt * 512:dt * 512 + ns],
                                 start=(dt == 0), stop=(dt == 7))
                    for dt in range(8):
                        mm = e.matmul(S1[:, 0:ns], c_onesb[:], rq16[:, dt * 512:dt * 512 + ns],
                                      start=(dt == 0), stop=(dt == 7))
                    return mm
                tk_sum = em.add("t", ln2sum, waits=[cps[-1]], inc=mt)
                sq2 = [tk_sum]
                lnm = t32(LN_M, 512)[0:1, :]
                lnv = t32(LN_V, 512)[0:1, :]
                lnt = t32(LN_T, 512)[0:1, :]
                lns = t32(LN_S, 512)[0:1, :]
                rstd_t = t32(LN_RSTD, 512)[0:1, :]
                em.add("v", lambda e, ns=ns: e.tensor_scalar(
                    lnm[:, 0:ns], S0[:, 0:ns], 1.0 / 1024, 0.0, ALU.mult, ALU.add),
                    waits=[tk_sum, sq2[-1]])
                em.add("v", lambda e, ns=ns: e.tensor_tensor(
                    lnt[:, 0:ns], lnm[:, 0:ns], lnm[:, 0:ns], ALU.mult), drain=True)
                em.add("v", lambda e, ns=ns: e.tensor_scalar(
                    lnv[:, 0:ns], S1[:, 0:ns], 1.0 / 1024, 0.0, ALU.mult, ALU.add))
                tkv_ = em.add("v", lambda e, ns=ns: e.tensor_tensor(
                    lnv[:, 0:ns], lnv[:, 0:ns], lnt[:, 0:ns], ALU.subtract), inc=mv, drain=True)
                em.add("s", lambda e, ns=ns: e.activation(
                    lns[:, 0:ns], lnv[:, 0:ns], AF.Ln, bias=epsv[0:1, :]),
                    waits=[tkv_], drain=True)
                tkr = em.add("s", lambda e, ns=ns: e.activation(
                    rstd_t[:, 0:ns], lns[:, 0:ns], AF.Exp, scale=-0.5),
                    drain=True, inc=ms)
                def ln2b(e, ns=ns):
                    e.matmul(P[2][:, 0:ns], c_ones1[:], lnm[:, 0:ns], start=True, stop=True)
                    return e.matmul(P[3][:, 0:ns], c_ones1[:], rstd_t[:, 0:ns],
                                    start=True, stop=True)
                tkb = em.add("t", ln2b, waits=[tkr], inc=mt)
                out_sb = t32(W_XO, 4096)
                for dt in range(8):
                    em.add("v", lambda e, dt=dt, ns=ns: e.tensor_tensor(
                        out_sb[:, dt * 512:dt * 512 + ns], r_sb[:, dt * 512:dt * 512 + ns],
                        P[2][:, 0:ns], ALU.subtract), waits=[tkb])
                    tka = em.add("v", lambda e, dt=dt, ns=ns: e.tensor_tensor(
                        out_sb[:, dt * 512:dt * 512 + ns], out_sb[:, dt * 512:dt * 512 + ns],
                        P[3][:, 0:ns], ALU.mult), inc=mv)
                tko = em.add("y", lambda e, coff=coff, nsoff=nsoff, ns=ns: e.dma_start(
                    bass.AP(y_h, coff + nsoff, [[8 * 2048, 128], [2048, 8], [1, ns]]),
                    bass.AP(ph32, W_XO, [[PH32_COLS, 128], [512, 8], [1, ns]])),
                    waits=[tka], inc=ds, n=16)
                em.add("v", lambda e: e.engine_nop(), waits=[tko])
                nsoff += ns

        em.add("g", lambda e: e.memzero(c_exp[:]), waits=[em.tick(ds), em.tick(dg)])

        with nc.Block() as blk:
            def runner(engname):
                def run(e):
                    seen = {k: 0 for k in SEMS}
                    for waits, inc, n, fn, drain in em.ops[engname]:
                        for (sk, v) in waits:
                            if v > seen[sk]:
                                e.wait_ge(sems[sk], v)
                                seen[sk] = v
                        if drain:
                            e.drain()
                        ins = fn(e)
                        if inc is not None:
                            ins.then_inc(sems[inc], n)
                return run
            blk.tensor(runner("t"))
            blk.scalar(runner("s"))
            blk.vector(runner("v"))
            blk.sync(runner("y"))
            blk.gpsimd(runner("g"))

    return nc


def kernel(**inputs):
    maps, chunk_lists = prep_inputs(inputs)
    nc = build_program()
    res = run_bass_kernel_spmd(nc, maps, core_ids=list(range(8)))
    out = np.zeros((B, N, D), dtype=np.float32)
    for c in range(8):
        y = res.results[c]["y"].reshape(128, 8, 2048)
        yt = y.transpose(2, 1, 0).reshape(2048, D)
        for s, ch in enumerate(chunk_lists[c]):
            out[c % 4, ch * 512:(ch + 1) * 512, :] = yt[s * 512:(s + 1) * 512]
    return out


# version 16
# speedup vs baseline: 1.0253x; 1.0253x over previous
import contextlib
import math
import numpy as np

import concourse.bass as bass
import concourse.mybir as mybir
from concourse.bass_utils import run_bass_kernel_spmd

F32 = mybir.dt.float32
BF16 = mybir.dt.bfloat16
I32 = mybir.dt.int32
AF = mybir.ActivationFunctionType
ALU = mybir.AluOpType

B, N, D, H, DH, NPAIR = 4, 4096, 1024, 16, 64, 32
SLOTS = 4
EXT = [32, 24, 16, 8]
CHUNKS_ROLE = ([7, 5, 3, 1], [6, 4, 2, 0])
NG = 4
FCH = [704, 704, 640]
FCH_OFF = [0, 704, 1408]
NSUB = {704: [512, 192], 640: [512, 128]}

RIEMANN = (14.134725142, 21.022039639, 25.01085758, 30.424876126, 32.935061588,
           37.586178159, 40.918719012, 43.327073281, 48.005150881, 49.773832478,
           52.970321478, 56.446247697, 59.347044003, 60.831778525, 65.112544048,
           67.079810529, 69.546401711, 72.067157674, 75.704690699, 77.144840069,
           79.33737502, 82.910380854, 84.735492981, 87.425274613, 88.809111208,
           92.491899271, 94.651344041, 95.870634228, 98.831194218, 101.317851006,
           103.72553804, 105.446623052)

TWO_PI = 2.0 * math.pi
CW3 = float(np.float32(TWO_PI - 6.28125))
EXP_SHIFT = -32.0

ENGS = ("t", "s", "v", "y", "g")
SEMS = ("ds", "dg", "mt", "ms", "mv")

O_KF = 0
O_QF = 8192
O_V16 = 12288
O_XC = 20992
O_WQ = 29184
O_EXP = 35328
O_ATT = 37376
O_WO = 8192
F_H = 0
F_C = 22528
F_XN = 45056
F_W1P = 50688
F_W2P = 52736
F_W3P = 30720
F_HP = 54784
PH16_COLS = 62976

W_XO = 0
W_R = 4096
OV = 8192
A_ROPE = OV
A_PK = OV + 4608
A_UBWB = OV
W_RSQ = OV
W_XNF = OV + 2048
LN_M = OV + 3072
LN_V = OV + 3584
LN_T = OV + 4096
LN_S = OV + 4608
LN_RSTD = OV + 5120
PH32_COLS = 13824


def _invfreq():
    g = np.asarray(RIEMANN[:NPAIR], dtype=np.float32)
    return (np.float32(1.0) / (g / g[0])).astype(np.float32)


def _pm_tiles(a):
    K, M = a.shape
    return np.ascontiguousarray(
        a.reshape(K // 128, 128, M).transpose(1, 0, 2).reshape(128, -1))


def _panel_tiles(aT):
    K, O = aT.shape
    r = aT.reshape(K // 128, 128, O // 128, 128).transpose(1, 2, 0, 3)
    return np.ascontiguousarray(r.reshape(128, -1))


def prep_inputs(inp):
    x = np.asarray(inp["L0_attn_x"])
    w_qkv = np.asarray(inp["L0_attn_w_qkv"])
    w_o = np.asarray(inp["L0_attn_w_o"])
    ls = np.asarray(inp["L0_attn_log_scale"])
    w1 = np.asarray(inp["L0_ffn_w1_w"]); s1 = np.asarray(inp["L0_ffn_w1_s"])
    w2 = np.asarray(inp["L0_ffn_w2_w"]); s2 = np.asarray(inp["L0_ffn_w2_s"])
    h1 = np.asarray(inp["L0_ffn_h1_w"]); sh1 = np.asarray(inp["L0_ffn_h1_s"])
    h2 = np.asarray(inp["L0_ffn_h2_w"]); sh2 = np.asarray(inp["L0_ffn_h2_s"])
    w3 = np.asarray(inp["L0_ffn_w3_w"]); s3 = np.asarray(inp["L0_ffn_w3_s"])

    invf = _invfreq()
    perm = np.concatenate([np.arange(0, 32, 2), np.arange(1, 32, 2),
                           np.arange(32, 64, 2), np.arange(33, 64, 2)])

    cols = []
    for g in range(NG):
        qc, kc, vc = [], [], []
        for h in range(4 * g, 4 * g + 4):
            qc.extend((h * DH + perm).tolist())
            kc.extend((1024 + h * DH + perm).tolist())
            vc.extend(range(2048 + h * DH, 2048 + (h + 1) * DH))
        cols.extend(qc + kc + vc)
    wsel = w_qkv[:, cols]
    wqkvg = np.ascontiguousarray(
        wsel.reshape(8, 128, NG, 768).transpose(1, 2, 0, 3).reshape(128, -1))

    shared = dict(
        wqkvg=wqkvg,
        wo=_pm_tiles(w_o),
        w1=_panel_tiles(np.ascontiguousarray(w1.T)),
        w2=_panel_tiles(np.ascontiguousarray(w2.T)),
        h1=_panel_tiles(np.ascontiguousarray(h1.T)),
        h2=_panel_tiles(np.ascontiguousarray(h2.T)),
        w3=_panel_tiles(np.ascontiguousarray(w3.T)),
        s1=np.ascontiguousarray(s1.reshape(32, 128).T),
        s2=np.ascontiguousarray(s2.reshape(32, 128).T),
        sh1=np.ascontiguousarray(sh1.reshape(32, 128).T),
        sh2=np.ascontiguousarray(sh2.reshape(32, 128).T),
        s3=np.ascontiguousarray(s3.reshape(8, 128).T),
    )
    lsp = np.empty((128, 8), dtype=np.float32)
    for hp in range(8):
        lsp[0:64, hp] = ls[2 * hp]
        lsp[64:128, hp] = ls[2 * hp + 1]
    shared["lsp"] = lsp
    pp_ = np.arange(128)
    prow = ((pp_ % 64) // 32) * 16 + (pp_ % 16)
    pos_all = np.arange(N, dtype=np.float32)
    shared["pk"] = np.ascontiguousarray(
        (pos_all[None, :] * invf[prow][:, None]).astype(np.float32))
    shared["qpos"] = np.ascontiguousarray(
        np.broadcast_to(np.arange(512, dtype=np.float32), (128, 512)))
    shared["pm1"] = np.ascontiguousarray(
        np.where(np.arange(128) % 32 < 16, 1.0, -1.0).astype(np.float32)[:, None])

    maps, chunk_lists = [], []
    for c in range(8):
        b = c % 4
        chunks = CHUNKS_ROLE[c // 4]
        chunk_lists.append(chunks)
        own = np.concatenate([np.arange(ch * 512, (ch + 1) * 512) for ch in chunks])
        m = dict(shared)
        m["xf"] = _pm_tiles(np.ascontiguousarray(x[b].T))
        m["xo"] = _pm_tiles(np.ascontiguousarray(x[b][own].T))
        m["pq"] = np.ascontiguousarray(
            (own.astype(np.float32)[None, :] * invf[prow][:, None]).astype(np.float32))
        thr = np.zeros((128, 32), dtype=np.float32)
        p_idx = np.arange(128)
        for s in range(SLOTS):
            for i in range(8):
                t = EXT[s] - 8 + i
                thr[:, s * 8 + i] = (t * 128 + p_idx) - chunks[s] * 512
        m["thr"] = thr
        maps.append(m)
    return maps, chunk_lists


class Em:
    def __init__(self):
        self.ops = {e: [] for e in ENGS}
        self.cnt = {s: 0 for s in SEMS}

    def add(self, eng, fn, waits=(), inc=None, n=1, drain=False):
        self.ops[eng].append((tuple(waits), inc, n, fn, drain))
        if inc is not None:
            self.cnt[inc] += n
            return (inc, self.cnt[inc])
        return None

    def tick(self, sem):
        return (sem, self.cnt[sem])


def build_program(dumps=()):
    nc = bass.Bass("TRN2", target_bir_lowering=False)
    em = Em()

    def din(name, shape, dt=F32):
        h = nc.dram_tensor(name, shape, dt, kind="ExternalInput")
        return h, h.ap()

    xf_h, d_xf = din("xf", [128, 8 * 4096])
    xo_h, d_xo = din("xo", [128, 8 * 2048])
    _, d_wqkv = din("wqkvg", [128, NG * 8 * 768])
    _, d_wo = din("wo", [128, 8 * 1024])
    _, d_w1 = din("w1", [128, 32 * 1024])
    _, d_w2 = din("w2", [128, 32 * 1024])
    _, d_h1 = din("h1", [128, 32 * 4096])
    _, d_h2 = din("h2", [128, 32 * 4096])
    _, d_w3 = din("w3", [128, 8 * 4096])
    _, d_s1 = din("s1", [128, 32])
    _, d_s2 = din("s2", [128, 32])
    _, d_sh1 = din("sh1", [128, 32])
    _, d_sh2 = din("sh2", [128, 32])
    _, d_s3 = din("s3", [128, 8])
    _, d_lsp = din("lsp", [128, 8])
    _, d_pk = din("pk", [128, 4096])
    _, d_pq = din("pq", [128, 2048])
    _, d_qpos = din("qpos", [128, 512])
    _, d_thr = din("thr", [128, 32])
    _, d_pm1 = din("pm1", [128, 1])

    y_h = nc.dram_tensor("y", [128, 8 * 2048], F32, kind="ExternalOutput")
    d_y = y_h.ap()
    xnf_kind = "ExternalOutput" if "xn" in dumps else "Internal"
    xnf_h = nc.dram_tensor("xnf_scr", [128, 8 * 2048], F32, kind=xnf_kind)
    d_xnf = xnf_h.ap()
    _b = lambda nm, cols: nc.dram_tensor(nm, [128, cols], BF16, kind="Internal").ap()
    d_w1b = _b("w1b_scr", 32 * 1024)
    d_w2b = _b("w2b_scr", 32 * 1024)
    d_h1b = _b("h1b_scr", 32 * 4096)
    d_h2b = _b("h2b_scr", 32 * 4096)
    d_w3b = _b("w3b_scr", 8 * 4096)
    d_xfb = _b("xfb_scr", 8 * 4096)
    d_xob = _b("xob_scr", 8 * 2048)

    d_dump = {}
    for name, cols in (("att", 16384), ("kf", 8192), ("qf", 4096), ("v16", 8192)):
        if name in dumps:
            d_dump[name] = nc.dram_tensor("dump_" + name, [128, cols], BF16,
                                          kind="ExternalOutput").ap()
    for name, cols in (("ph32", PH32_COLS), ("cs", 8)):
        if name in dumps:
            d_dump[name] = nc.dram_tensor("dump_" + name, [128, cols], F32,
                                          kind="ExternalOutput").ap()

    es = contextlib.ExitStack()
    with es:
        E = es.enter_context
        ph16 = E(nc.sbuf_tensor("ph16", [128, PH16_COLS], BF16))
        ph32 = E(nc.sbuf_tensor("ph32", [128, PH32_COLS], F32))
        i32b = E(nc.sbuf_tensor("i32b", [128, 512], I32))
        c_qpos = E(nc.sbuf_tensor("c_qpos", [128, 512], F32))
        c_thr = E(nc.sbuf_tensor("c_thr", [128, 32], F32))
        c_lsp = E(nc.sbuf_tensor("c_lsp", [128, 8], F32))
        c_scale = E(nc.sbuf_tensor("c_scale", [128, 8], F32))
        c_s1 = E(nc.sbuf_tensor("c_s1", [128, 32], F32))
        c_s2 = E(nc.sbuf_tensor("c_s2", [128, 32], F32))
        c_sh1 = E(nc.sbuf_tensor("c_sh1", [128, 32], F32))
        c_sh2 = E(nc.sbuf_tensor("c_sh2", [128, 32], F32))
        c_s3 = E(nc.sbuf_tensor("c_s3", [128, 8], F32))
        c_misc = E(nc.sbuf_tensor("c_misc", [128, 8], F32))
        c_ones1 = E(nc.sbuf_tensor("c_ones1", [1, 128], F32))
        c_onesb = E(nc.sbuf_tensor("c_onesb", [128, 1], BF16))
        c_exp = E(nc.sbuf_tensor("c_exp", [128, 8], F32))

        P = [E(nc.psum_tensor(f"P{i}", [128, 512], F32)) for i in range(6)]
        S0 = E(nc.psum_tensor("S0", [1, 512], F32))
        S1 = E(nc.psum_tensor("S1", [1, 512], F32))

        sems = {k: E(nc.semaphore("sem_" + k)) for k in SEMS}

        def t16(off, ln):
            return ph16[:, off:off + ln]

        def t32(off, ln):
            return ph32[:, off:off + ln]

        ds, dg, mt, ms, mv = SEMS

        for dst, src in ((c_qpos, d_qpos), (c_thr, d_thr), (c_lsp, d_lsp),
                         (c_s1, d_s1), (c_s2, d_s2), (c_sh1, d_sh1),
                         (c_sh2, d_sh2), (c_s3, d_s3)):
            em.add("y", lambda e, dst=dst, src=src: e.dma_start(dst[:], src),
                   inc=ds, n=16)
        em.add("y", lambda e: e.dma_start(c_misc[:, 0:1], d_pm1), inc=ds, n=16)
        t_setup = em.tick(ds)

        em.add("v", lambda e: e.memset(c_misc[:, 1:2], EXP_SHIFT))
        em.add("v", lambda e: e.memset(c_misc[:, 2:3], 1e-5))
        em.add("v", lambda e: e.memset(c_misc[:, 3:4], math.pi / 2))
        em.add("v", lambda e: e.memset(c_misc[:, 4:5], 1.0))
        em.add("v", lambda e: e.memset(c_ones1[:], 1.0))
        em.add("v", lambda e: e.memset(c_onesb[:], 1.0))
        em.add("v", lambda e: e.memset(c_exp[:], 1.0), waits=[t_setup])
        for n_ in range(12, 0, -1):
            em.add("v", lambda e, n_=n_: e.tensor_scalar(
                c_scale[:], c_lsp[:], 1.0 / n_, 0.0, ALU.mult, ALU.add), drain=True)
            em.add("v", lambda e: e.tensor_tensor(c_exp[:], c_scale[:], c_exp[:], ALU.mult),
                   drain=True)
            em.add("v", lambda e: e.tensor_scalar(c_exp[:], c_exp[:], 1.0, 1.0, ALU.mult,
                                                  ALU.add), drain=True)
        t_scale = em.add("v", lambda e: e.tensor_copy(c_scale[:], c_exp[:]), inc=mv,
                         drain=True)

        pm1 = c_misc[:, 0:1]
        neg32 = c_misc[:, 1:2]
        epsv = c_misc[:, 2:3]
        hpiv = c_misc[:, 3:4]
        ones32 = c_misc[:, 4:5]

        kf = lambda pp: O_KF + pp * 4096
        qf = lambda pp: O_QF + pp * 2048

        def rope_tables(hp_global, src_sb, w_dma):
            th = t32(A_ROPE, 512)
            cosb = t32(A_ROPE + 512, 512)
            sinb = t32(A_ROPE + 1024, 512)
            nf = t32(A_ROPE + 1536, 512)
            em.add("s", lambda e: e.activation(
                th, src_sb, AF.Copy, scale=c_scale[:, hp_global:hp_global + 1]),
                waits=[t_scale, w_dma, em.tick(mv)], drain=True)
            em.add("s", lambda e: e.activation(i32b[:], th, AF.Copy, scale=1.0 / TWO_PI))
            tk_n = em.add("s", lambda e: e.copy(nf, i32b[:]), inc=ms)
            em.add("v", lambda e: e.scalar_tensor_tensor(th, nf, -6.28125, th,
                                                         ALU.mult, ALU.add), waits=[tk_n])
            em.add("v", lambda e: e.scalar_tensor_tensor(th, nf, -CW3, th, ALU.mult, ALU.add))
            tk_r = em.add("v", lambda e: e.tensor_scalar(
                cosb, th, math.pi / 2, 0.0, ALU.is_gt, ALU.add), inc=mv)
            em.add("s", lambda e: e.activation(sinb, th, AF.Sin, scale=pm1), waits=[tk_r])
            tk_rc = em.add("v", lambda e: e.scalar_tensor_tensor(
                cosb, cosb, -TWO_PI, th, ALU.mult, ALU.add), inc=mv)
            tk_cos = em.add("s", lambda e: e.activation(cosb, cosb, AF.Sin, bias=hpiv),
                            waits=[tk_rc], inc=ms)
            return cosb, sinb, tk_cos

        SWAP16 = list(range(16, 32)) + list(range(0, 16))

        def rope_apply(psrc, cosb, sinb, dst16, tk_mm, tk_tab):
            swp = t32(A_ROPE + 2048, 512)
            em.add("v", lambda e: e.tensor_tensor(swp, psrc[:], sinb, ALU.mult),
                   waits=[tk_mm, tk_tab])
            em.add("v", lambda e: e.stream_shuffle(swp, swp, SWAP16), drain=True)
            em.add("v", lambda e: e.tensor_tensor(cosb, psrc[:], cosb, ALU.mult))
            return em.add("v", lambda e: e.tensor_tensor(dst16, cosb, swp, ALU.add),
                          inc=mv, drain=True)

        NKT = 8

        tk_vones = em.add("v", lambda e: e.memset(
            bass.AP(ph16, O_V16 + 64, [[PH16_COLS, 128], [68, 128]]), 1.0), inc=mv)

        rope_prev_apply = [None, None]
        rope_ip = [0]

        def rope_tables(hp_global, src_sb, w_dma):
            ip = rope_ip[0] % 2
            rope_ip[0] += 1
            base = A_ROPE + ip * 2048
            th = t32(base, 512)
            cosb = t32(base + 512, 512)
            sinb = t32(base + 1024, 512)
            nf = t32(base + 1536, 512)
            w0 = [t_scale, w_dma]
            if rope_prev_apply[ip] is not None:
                w0.append(rope_prev_apply[ip])
            em.add("s", lambda e: e.activation(
                th, src_sb, AF.Copy, scale=c_scale[:, hp_global:hp_global + 1]),
                waits=w0)
            em.add("s", lambda e: e.activation(i32b[:], th, AF.Copy, scale=1.0 / TWO_PI))
            tk_n = em.add("s", lambda e: e.copy(nf, i32b[:]), inc=ms)
            em.add("v", lambda e: e.scalar_tensor_tensor(th, nf, -6.28125, th,
                                                         ALU.mult, ALU.add), waits=[tk_n])
            em.add("v", lambda e: e.scalar_tensor_tensor(th, nf, -CW3, th, ALU.mult, ALU.add))
            tk_r = em.add("v", lambda e: e.tensor_scalar(
                cosb, th, math.pi / 2, 0.0, ALU.is_gt, ALU.add), inc=mv)
            em.add("s", lambda e: e.activation(sinb, th, AF.Sin, scale=pm1), waits=[tk_r])
            tk_rc = em.add("v", lambda e: e.scalar_tensor_tensor(
                cosb, cosb, -TWO_PI, th, ALU.mult, ALU.add), inc=mv)
            tk_cos = em.add("s", lambda e: e.activation(cosb, cosb, AF.Sin, bias=hpiv),
                            waits=[tk_rc], inc=ms)
            return ip, cosb, sinb, tk_cos

        SWAP16 = list(range(16, 32)) + list(range(0, 16))

        def rope_apply(ip, psrc, cosb, sinb, dst16, tk_mm, tk_tab):
            swp = t32(A_ROPE + 4096, 512)
            em.add("v", lambda e: e.tensor_tensor(swp, psrc[:], sinb, ALU.mult),
                   waits=[tk_mm, tk_tab])
            em.add("v", lambda e: e.stream_shuffle(swp, swp, SWAP16), drain=True)
            em.add("v", lambda e: e.tensor_tensor(cosb, psrc[:], cosb, ALU.mult))
            tk = em.add("v", lambda e: e.tensor_tensor(dst16, cosb, swp, ALU.add),
                        inc=mv, drain=True)
            rope_prev_apply[ip] = tk
            return tk

        NKT = 8
        for g in range(NG):
            wq16 = t16(O_WQ, 8 * 768)
            tk_wq = em.add("g", lambda e, g=g: e.dma_start(
                wq16, d_wqkv[:, g * 8 * 768:(g + 1) * 8 * 768]),
                waits=[em.tick(mt)], inc=dg, n=16)

            xc_guard = {}
            kq_psum_guard = [None, None]
            vevac_tick = {}
            for tc in range(8):
                xc = t16(O_XC + (tc % 2) * 4096, 4096)
                w = list(xc_guard[tc - 2]) if tc >= 2 else [em.tick(mt)]
                if g == 0:
                    tk_x = em.add("g", lambda e, tc=tc, xc=xc: e.dma_start(
                        xc, bass.AP(xf_h, tc * 512, [[8 * 4096, 128], [4096, 8], [1, 512]])),
                        waits=w, inc=dg, n=16)
                    em.add("y", lambda e, tc=tc, xc=xc: e.dma_start(
                        d_xfb[:, tc * 4096:(tc + 1) * 4096], xc), waits=[tk_x], inc=ds, n=16)
                else:
                    tk_x = em.add("y", lambda e, tc=tc, xc=xc: e.dma_start(
                        xc, d_xfb[:, tc * 4096:(tc + 1) * 4096]), waits=w, inc=ds, n=16)
                pk_sb = t32(A_PK + (tc % 2) * 512, 512)
                tk_pk = em.add("y", lambda e, tc=tc, pk_sb=pk_sb: e.dma_start(
                    pk_sb, d_pk[:, tc * 512:(tc + 1) * 512]),
                    waits=[em.tick(ms)], inc=ds, n=16)
                for pp in range(2):
                    hp_global = 2 * g + pp
                    ps_k = P[pp]
                    wk = [tk_x, tk_wq]
                    if kq_psum_guard[pp] is not None:
                        wk.append(kq_psum_guard[pp])
                    else:
                        wk.append(em.tick(ms))
                        wk.append(em.tick(mv))
                    def kmm(e, pp=pp, xc=xc, ps_k=ps_k):
                        for kt in range(NKT):
                            mm = e.matmul(
                                ps_k[:],
                                wq16[:, kt * 768 + 256 + pp * 128:kt * 768 + 256 + (pp + 1) * 128],
                                xc[:, kt * 512:(kt + 1) * 512],
                                start=(kt == 0), stop=(kt == NKT - 1))
                        return mm
                    tkm = em.add("t", kmm, waits=wk, inc=mt)
                    ip, cosb, sinb, tk_tab = rope_tables(hp_global, pk_sb, tk_pk)
                    kq_psum_guard[pp] = rope_apply(
                        ip, ps_k, cosb, sinb, t16(kf(pp) + tc * 512, 512), tkm, tk_tab)
                for tt in range(4):
                    ps_v = P[2 + tt % 2]
                    tt_global = tc * 4 + tt
                    wv = [tk_x, tk_wq]
                    prev = vevac_tick.get(tt_global - 2)
                    if prev is not None:
                        wv.append(prev)
                    else:
                        wv.append(em.tick(ms))
                        wv.append(em.tick(mv))
                    def vmm(e, xc=xc, tt=tt, ps_v=ps_v):
                        for kt in range(NKT):
                            mm = e.matmul(
                                ps_v[:, 0:256],
                                xc[:, kt * 512 + tt * 128:kt * 512 + (tt + 1) * 128],
                                wq16[:, kt * 768 + 512:kt * 768 + 768],
                                start=(kt == 0), stop=(kt == NKT - 1))
                        return mm
                    tkv = em.add("t", vmm, waits=wv, inc=mt)
                    def vevac(e, ps_v=ps_v, tt_global=tt_global):
                        for hg in range(4):
                            ins = e.tensor_copy(t16(O_V16 + tt_global * 272 + hg * 68, 64),
                                                ps_v[:, hg * 64:(hg + 1) * 64])
                        return ins
                    vevac_tick[tt_global] = em.add("v", vevac, waits=[tkv], inc=mv)
                xc_guard[tc] = (em.tick(mt),) if g > 0 else (em.tick(mt), em.tick(ds))
            for qc in range(4):
                xc = t16(O_XC + (qc % 2) * 4096, 4096)
                if qc < 2:
                    w = list(xc_guard[6 + qc])
                else:
                    w = [em.tick(mt)] if g > 0 else [em.tick(mt), em.tick(ds)]
                if g == 0:
                    tk_x = em.add("g", lambda e, qc=qc, xc=xc: e.dma_start(
                        xc, bass.AP(xo_h, qc * 512, [[8 * 2048, 128], [2048, 8], [1, 512]])),
                        waits=w, inc=dg, n=16)
                    em.add("y", lambda e, qc=qc, xc=xc: e.dma_start(
                        d_xob[:, qc * 4096:(qc + 1) * 4096], xc), waits=[tk_x], inc=ds, n=16)
                else:
                    tk_x = em.add("y", lambda e, qc=qc, xc=xc: e.dma_start(
                        xc, d_xob[:, qc * 4096:(qc + 1) * 4096]), waits=w, inc=ds, n=16)
                pq_sb = t32(A_PK + (qc % 2) * 512, 512)
                tk_pq = em.add("y", lambda e, qc=qc, pq_sb=pq_sb: e.dma_start(
                    pq_sb, d_pq[:, qc * 512:(qc + 1) * 512]),
                    waits=[em.tick(ms)], inc=ds, n=16)
                for pp in range(2):
                    hp_global = 2 * g + pp
                    ps_q = P[pp]
                    def qmm(e, pp=pp, xc=xc, ps_q=ps_q):
                        for kt in range(NKT):
                            mm = e.matmul(
                                ps_q[:],
                                wq16[:, kt * 768 + pp * 128:kt * 768 + (pp + 1) * 128],
                                xc[:, kt * 512:(kt + 1) * 512],
                                start=(kt == 0), stop=(kt == NKT - 1))
                        return mm
                    tkm = em.add("t", qmm, waits=[tk_x, kq_psum_guard[pp]], inc=mt)
                    ip, cosb, sinb, tk_tab = rope_tables(hp_global, pq_sb, tk_pq)
                    kq_psum_guard[pp] = rope_apply(
                        ip, ps_q, cosb, sinb, t16(qf(pp) + qc * 512, 512), tkm, tk_tab)

            fin_prev = {"tkc": None, "norm": None, "exp_last": None}
            for s in range(SLOTS):
                ext = EXT[s]
                for pp in range(2):
                    exp_tick = {}
                    av_tick = {}
                    sc_tick = {}

                    def emit_sc(t, pp=pp, s=s, ext=ext):
                        pb = t % 2
                        psA = [P[0], P[4]][pb]
                        psB = [P[1], P[5]][pb]
                        def smm(e, pp=pp, s=s, t=t, psA=psA, psB=psB):
                            e.matmul(psA[:], t16(kf(pp) + t * 128, 128)[0:64, :],
                                     t16(qf(pp) + s * 512, 512)[0:64, :],
                                     start=True, stop=True)
                            return e.matmul(psB[:], t16(kf(pp) + t * 128, 128)[64:128, :],
                                            t16(qf(pp) + s * 512, 512)[64:128, :],
                                            start=True, stop=True)
                        if t >= 2:
                            w = [exp_tick[t - 2]]
                        elif fin_prev["tkc"] is not None:
                            w = [fin_prev["tkc"], fin_prev["exp_last"]]
                        else:
                            w = [em.tick(ms), em.tick(mv)]
                        sc_tick[t] = em.add("t", smm, waits=w, inc=mt)

                    emit_sc(0)
                    if ext > 1:
                        emit_sc(1)
                    for t in range(ext):
                        pb = t % 2
                        psA = [P[0], P[4]][pb]
                        psB = [P[1], P[5]][pb]
                        expA = t16(O_EXP + pb * 512, 512)
                        expB = t16(O_EXP + 1024 + pb * 512, 512)
                        wE = [sc_tick[t]]
                        if t >= 2:
                            wE.append(av_tick[t - 2])
                        tkeA = em.add("s", lambda e, psA=psA, expA=expA: e.activation(
                            expA, psA[:], AF.Exp, bias=neg32, scale=0.125),
                            waits=wE, inc=ms)
                        tkeB = em.add("s", lambda e, psB=psB, expB=expB: e.activation(
                            expB, psB[:], AF.Exp, bias=neg32, scale=0.125), inc=ms)
                        exp_tick[t] = tkeB
                        wavA, wavB = [tkeA], [tkeB]
                        if t >= ext - 8:
                            col = s * 8 + (t - (ext - 8))
                            tkMA = em.add("v", lambda e, expA=expA, col=col: e.scalar_tensor_tensor(
                                expA, c_qpos[:], c_thr[:, col:col + 1], expA,
                                ALU.is_ge, ALU.mult), waits=[tkeA], inc=mv)
                            tkMB = em.add("v", lambda e, expB=expB, col=col: e.scalar_tensor_tensor(
                                expB, c_qpos[:], c_thr[:, col:col + 1], expB,
                                ALU.is_ge, ALU.mult), waits=[tkeB], inc=mv)
                            wavA, wavB = [tkMA], [tkMB]
                        if t == 0:
                            wavA.append(fin_prev["norm"] if fin_prev["norm"] is not None
                                        else em.tick(mv))
                            wavA.append(tk_vones)
                        if t + 2 < ext:
                            emit_sc(t + 2)
                        tka = em.add("t", lambda e, t=t, pp=pp, expA=expA, ext=ext: e.matmul(
                            P[2][0:65, :], t16(O_V16 + t * 272 + (2 * pp) * 68, 65),
                            expA, start=(t == 0), stop=(t == ext - 1)),
                            waits=wavA, inc=mt)
                        av_tick[t] = em.add("t", lambda e, t=t, pp=pp, expB=expB, ext=ext: e.matmul(
                            P[3][0:65, :], t16(O_V16 + t * 272 + (2 * pp + 1) * 68, 65),
                            expB, start=(t == 0), stop=(t == ext - 1)),
                            waits=wavB, inc=mt)
                    rbA = t32(W_R, 512)
                    rbB = t32(W_R + 1024, 512)
                    nt = t32(W_R + 2048, 512)
                    tk_last = av_tick[ext - 1]
                    em.add("s", lambda e, rbA=rbA: e.activation(
                        rbA[0:1, :], P[2][64:65, :], AF.Ln), waits=[tk_last], drain=True)
                    em.add("s", lambda e, rbA=rbA: e.activation(
                        rbA[0:1, :], rbA[0:1, :], AF.Exp, scale=-1.0))
                    em.add("s", lambda e, rbB=rbB: e.activation(
                        rbB[0:1, :], P[3][64:65, :], AF.Ln))
                    tkrec = em.add("s", lambda e, rbB=rbB: e.activation(
                        rbB[0:1, :], rbB[0:1, :], AF.Exp, scale=-1.0), inc=ms)
                    def bcast(e, rbA=rbA, rbB=rbB):
                        e.matmul(P[0][0:64, :], c_ones1[:, 0:64], rbA[0:1, :],
                                 start=True, stop=True, tile_position=(0, 0))
                        return e.matmul(P[0][64:128, :], c_ones1[:, 0:64], rbB[0:1, :],
                                        start=True, stop=True, tile_position=(0, 64))
                    tkb = em.add("t", bcast, waits=[tkrec, em.tick(ms)], inc=mt)
                    tkc = em.add("s", lambda e, rbA=rbA: e.copy(rbA, P[0][:]),
                                 waits=[tkb], inc=ms)
                    pair_idx = 2 * g + pp
                    em.add("v", lambda e, pair_idx=pair_idx, s=s, rbA=rbA: e.tensor_tensor(
                        t16(O_ATT + pair_idx * 2048 + s * 512, 512)[0:64, :],
                        P[2][0:64, :], rbA[0:64, :], ALU.mult), waits=[tkc])
                    tk_norm = em.add("v", lambda e, pair_idx=pair_idx, s=s, rbA=rbA:
                                     e.tensor_tensor(
                                         t16(O_ATT + pair_idx * 2048 + s * 512, 512)[64:128, :],
                                         P[3][0:64, :], rbA[64:128, :], ALU.mult), inc=mv)
                    fin_prev = {"tkc": tkc, "norm": tk_norm, "exp_last": exp_tick[ext - 1]}

        t_att = em.tick(mv)
        for nm, off, cols in (("att", O_ATT, 16384), ("kf", O_KF, 8192),
                              ("qf", O_QF, 4096), ("v16", O_V16, 8192)):
            if nm in d_dump:
                em.add("y", lambda e, nm=nm, off=off, cols=cols: e.dma_start(
                    d_dump[nm], t16(off, cols)), waits=[t_att], inc=ds, n=16)
        if "ph32" in d_dump:
            em.add("y", lambda e: e.dma_start(d_dump["ph32"], ph32[:]),
                   waits=[t_att], inc=ds, n=16)
        if "cs" in d_dump:
            em.add("y", lambda e: e.dma_start(d_dump["cs"], c_scale[:]),
                   waits=[t_att], inc=ds, n=16)

        wo16 = t16(O_WO, 8192)
        tk_wo = em.add("g", lambda e: e.dma_start(wo16, d_wo),
                       waits=[em.tick(mt)], inc=dg, n=16)
        r_sb = t32(W_R, 4096)
        for qc in range(4):
            xo_sb = t32(W_XO, 4096)
            w_prev = [em.tick(ds)] if qc == 0 else [tk_spill]
            tk_xo = em.add("y", lambda e, qc=qc: e.dma_start(
                xo_sb, bass.AP(xo_h, qc * 512, [[8 * 2048, 128], [2048, 8], [1, 512]])),
                waits=[em.tick(mv)], inc=ds, n=16)
            ev = []
            for dt in range(8):
                psy = P[dt % 2]
                def womm(e, dt=dt, qc=qc, psy=psy):
                    for kt in range(8):
                        mm = e.matmul(psy[:],
                                      wo16[:, kt * 1024 + dt * 128:kt * 1024 + (dt + 1) * 128],
                                      t16(O_ATT + kt * 2048 + qc * 512, 512),
                                      start=(kt == 0), stop=(kt == 7))
                    return mm
                w = [tk_wo, t_att]
                if dt >= 2:
                    w.append(ev[dt - 2])
                tkm = em.add("t", womm, waits=w, inc=mt)
                tke = em.add("v", lambda e, dt=dt, psy=psy: e.tensor_tensor(
                    r_sb[:, dt * 512:(dt + 1) * 512], psy[:],
                    xo_sb[:, dt * 512:(dt + 1) * 512], ALU.add),
                    waits=[tkm, tk_xo], inc=mv)
                ev.append(tke)
            r16 = t16(F_H, 4096)
            rq16 = t16(F_H + 4096, 4096)
            cps = []
            for dt in range(8):
                em.add("s", lambda e, dt=dt: e.copy(
                    r16[:, dt * 512:(dt + 1) * 512], r_sb[:, dt * 512:(dt + 1) * 512]),
                    waits=[ev[dt]])
                tkc_ = em.add("s", lambda e, dt=dt: e.activation(
                    rq16[:, dt * 512:(dt + 1) * 512], r_sb[:, dt * 512:(dt + 1) * 512],
                    AF.Square), inc=ms)
                cps.append(tkc_)
            def lnsum(e):
                for dt in range(8):
                    e.matmul(S0[:], c_onesb[:], r16[:, dt * 512:(dt + 1) * 512],
                             start=(dt == 0), stop=(dt == 7))
                for dt in range(8):
                    mm = e.matmul(S1[:], c_onesb[:], rq16[:, dt * 512:(dt + 1) * 512],
                                  start=(dt == 0), stop=(dt == 7))
                return mm
            tk_sum = em.add("t", lnsum, waits=[cps[-1]], inc=mt)
            sq = [tk_sum]
            lnm = t32(LN_M, 512)[0:1, :]
            lnv = t32(LN_V, 512)[0:1, :]
            lnt = t32(LN_T, 512)[0:1, :]
            lns = t32(LN_S, 512)[0:1, :]
            rstd_t = t32(LN_RSTD, 512)[0:1, :]
            em.add("v", lambda e: e.tensor_scalar(lnm, S0[:], 1.0 / 1024, 0.0,
                                                  ALU.mult, ALU.add), waits=[tk_sum, sq[-1]])
            em.add("v", lambda e: e.tensor_tensor(lnt, lnm, lnm, ALU.mult), drain=True)
            em.add("v", lambda e: e.tensor_scalar(lnv, S1[:], 1.0 / 1024, 0.0, ALU.mult, ALU.add))
            tkv_ = em.add("v", lambda e: e.tensor_tensor(lnv, lnv, lnt,
                                                         ALU.subtract), inc=mv, drain=True)
            em.add("s", lambda e: e.activation(lns, lnv, AF.Ln,
                                               bias=epsv[0:1, :]),
                   waits=[tkv_], drain=True)
            tkr = em.add("s", lambda e: e.activation(rstd_t, lns, AF.Exp, scale=-0.5),
                         drain=True, inc=ms)
            def lnb(e):
                e.matmul(P[2][:], c_ones1[:], lnm, start=True, stop=True)
                return e.matmul(P[3][:], c_ones1[:], rstd_t, start=True, stop=True)
            tkb = em.add("t", lnb, waits=[tkr], inc=mt)
            for dt in range(8):
                em.add("v", lambda e, dt=dt: e.tensor_tensor(
                    xo_sb[:, dt * 512:(dt + 1) * 512], r_sb[:, dt * 512:(dt + 1) * 512],
                    P[2][:], ALU.subtract), waits=[tkb])
                tka = em.add("v", lambda e, dt=dt: e.tensor_tensor(
                    xo_sb[:, dt * 512:(dt + 1) * 512], xo_sb[:, dt * 512:(dt + 1) * 512],
                    P[3][:], ALU.mult), inc=mv)
            tk_spill = em.add("y", lambda e, qc=qc: e.dma_start(
                bass.AP(xnf_h, qc * 512, [[8 * 2048, 128], [2048, 8], [1, 512]]),
                xo_sb), waits=[tka], inc=ds, n=16)
        t_w = em.tick(ds)

        prev_f1_mt = None
        prev_f3_mt = None
        for ci, CH in enumerate(FCH):
            coff = FCH_OFF[ci]
            xn16 = t16(F_XN, 8 * CH)
            wxn = [t_w] if prev_f1_mt is None else [t_w, prev_f1_mt]
            tk_xn = em.add("g", lambda e, coff=coff, CH=CH, xn16=xn16: e.dma_start(
                xn16, bass.AP(xnf_h, coff, [[8 * 2048, 128], [2048, 8], [1, CH]])),
                waits=wxn, inc=dg, n=16)

            f1_guard = {}
            h_tick = {}
            f1_evac = {}
            for ot in range(32):
                pbuf = ot % 2
                w1p = t16(F_W1P + pbuf * 1024, 1024)
                w2p = t16(F_W2P + pbuf * 1024, 1024)
                if ci == 0:
                    wdep = [f1_guard[ot - 2]] if ot >= 2 else []
                else:
                    wdep = [f1_guard[ot - 2]] if ot >= 2 else [prev_f1_mt]
                if ci == 0:
                    tk1 = em.add("g", lambda e, ot=ot, w1p=w1p: e.dma_start(
                        w1p, d_w1[:, ot * 1024:(ot + 1) * 1024]), waits=wdep, inc=dg, n=16)
                    tkw = em.add("g", lambda e, ot=ot, w2p=w2p: e.dma_start(
                        w2p, d_w2[:, ot * 1024:(ot + 1) * 1024]), inc=dg, n=16)
                    em.add("y", lambda e, ot=ot, w1p=w1p: e.dma_start(
                        d_w1b[:, ot * 1024:(ot + 1) * 1024], w1p), waits=[tk1], inc=ds, n=16)
                    em.add("y", lambda e, ot=ot, w2p=w2p: e.dma_start(
                        d_w2b[:, ot * 1024:(ot + 1) * 1024], w2p), waits=[tkw], inc=ds, n=16)
                else:
                    tk1 = em.add("y", lambda e, ot=ot, w1p=w1p: e.dma_start(
                        w1p, d_w1b[:, ot * 1024:(ot + 1) * 1024]), waits=wdep, inc=ds, n=16)
                    tkw = em.add("y", lambda e, ot=ot, w2p=w2p: e.dma_start(
                        w2p, d_w2b[:, ot * 1024:(ot + 1) * 1024]), inc=ds, n=16)
                nsoff = 0
                for nsi, ns in enumerate(NSUB[CH]):
                    par = (ot * 2 + nsi) % 2
                    sub_idx = ot * len(NSUB[CH]) + nsi
                    psa = P[0 + par]
                    psb = P[2 + par]
                    ub = t32(A_UBWB + par * 1024, 512)
                    wb = t32(A_UBWB + par * 1024 + 512, 512)
                    def f1mm(e, w1p=w1p, w2p=w2p, xn16=xn16, nsoff=nsoff, ns=ns,
                             psa=psa, psb=psb, CH=CH):
                        for kt in range(8):
                            e.matmul(psa[:, 0:ns], w1p[:, kt * 128:(kt + 1) * 128],
                                     xn16[:, kt * CH + nsoff:kt * CH + nsoff + ns],
                                     start=(kt == 0), stop=(kt == 7))
                        for kt in range(8):
                            mm = e.matmul(psb[:, 0:ns], w2p[:, kt * 128:(kt + 1) * 128],
                                          xn16[:, kt * CH + nsoff:kt * CH + nsoff + ns],
                                          start=(kt == 0), stop=(kt == 7))
                        return mm
                    wmm1 = [tk1, tkw, tk_xn]
                    if f1_evac.get(sub_idx - 2) is not None:
                        wmm1.append(f1_evac[sub_idx - 2])
                    else:
                        wmm1.append(em.tick(mv))
                    tkm = em.add("t", f1mm, waits=wmm1, inc=mt)
                    wS = [tkm]
                    hist = h_tick.get((ot * 2 + nsi) - 2)
                    if hist is not None:
                        wS.append(hist)
                    em.add("s", lambda e, psa=psa, ot=ot, ns=ns, ub=ub: e.activation(
                        ub[:, 0:ns], psa[:, 0:ns], AF.Silu, scale=c_s1[:, ot:ot + 1]),
                        waits=wS)
                    tke = em.add("s", lambda e, psb=psb, ot=ot, ns=ns, wb=wb: e.activation(
                        wb[:, 0:ns], psb[:, 0:ns], AF.Copy, scale=c_s2[:, ot:ot + 1]),
                        inc=ms)
                    f1_evac[sub_idx] = tke
                    tkh = em.add("v", lambda e, ot=ot, ns=ns, nsoff=nsoff, CH=CH, ub=ub,
                                 wb=wb: e.tensor_tensor(
                                     t16(F_H + ot * CH + nsoff, ns), ub[:, 0:ns],
                                     wb[:, 0:ns], ALU.mult), waits=[tke], inc=mv)
                    h_tick[ot * 2 + nsi] = tkh
                    nsoff += ns
                f1_guard[ot] = em.tick(mv)
            prev_f1_mt = em.tick(mt)

            for phase_i, (dsrc, dbsc, src_off, dst_off, ssc) in enumerate(
                    ((d_h1, d_h1b, F_H, F_C, c_sh1), (d_h2, d_h2b, F_C, F_H, c_sh2))):
                pg = {}
                t_src_ready = em.tick(mv) if phase_i == 0 else em.tick(ms)
                for ot in range(32):
                    pbuf = ot % 2
                    hp = t16(F_HP + pbuf * 4096, 4096)
                    if ci == 0 and phase_i == 0:
                        wdep = [pg[ot - 2]] if ot >= 2 else []
                    elif phase_i == 0 and prev_f3_mt is not None:
                        wdep = [pg[ot - 2]] if ot >= 2 else [prev_f3_mt]
                    else:
                        wdep = [pg[ot - 2]] if ot >= 2 else [em.tick(mt), em.tick(mv)]
                    if ci == 0:
                        tkp = em.add("g", lambda e, ot=ot, hp=hp, dsrc=dsrc: e.dma_start(
                            hp, dsrc[:, ot * 4096:(ot + 1) * 4096]), waits=wdep,
                            inc=dg, n=16)
                        em.add("y", lambda e, ot=ot, hp=hp, dbsc=dbsc: e.dma_start(
                            dbsc[:, ot * 4096:(ot + 1) * 4096], hp), waits=[tkp],
                            inc=ds, n=16)
                    else:
                        tkp = em.add("y", lambda e, ot=ot, hp=hp, dbsc=dbsc: e.dma_start(
                            hp, dbsc[:, ot * 4096:(ot + 1) * 4096]), waits=wdep,
                            inc=ds, n=16)
                    nsoff = 0
                    prev_evac = None
                    for ns in NSUB[CH]:
                        psc_ = P[0 + (ot % 2)]
                        wmm = [tkp, t_src_ready]
                        if ot >= 2:
                            wmm.append(pg[ot - 2])
                        if prev_evac is not None:
                            wmm.append(prev_evac)
                        def f2mm(e, hp=hp, src_off=src_off, nsoff=nsoff, ns=ns,
                                 psc_=psc_, CH=CH):
                            for kt in range(32):
                                mm = e.matmul(psc_[:, 0:ns], hp[:, kt * 128:(kt + 1) * 128],
                                              t16(src_off + kt * CH + nsoff, ns),
                                              start=(kt == 0), stop=(kt == 31))
                            return mm
                        tkm = em.add("t", f2mm, waits=wmm, inc=mt)
                        prev_evac = em.add("s", lambda e, psc_=psc_, ot=ot, ns=ns,
                                           dst_off=dst_off, nsoff=nsoff, CH=CH, ssc=ssc:
                                           e.activation(
                                               t16(dst_off + ot * CH + nsoff, ns),
                                               psc_[:, 0:ns],
                                               AF.Gelu, scale=ssc[:, ot:ot + 1]),
                                           waits=[tkm], inc=ms)
                        nsoff += ns
                    pg[ot] = em.tick(ms)
                if phase_i == 1:
                    prev_f3_mt = em.tick(mt)

            nsoff = 0
            t_d_ready = None
            for nsi, ns in enumerate(NSUB[CH]):
                if t_d_ready is None:
                    t_d_ready = em.tick(ms)
                ev4 = {}
                for ot in range(8):
                    pbuf = ot % 2
                    w3p = t16(F_W3P + pbuf * 4096, 4096)
                    wdep = [ev4[ot - 2]] if ot >= 2 else [em.tick(mt), em.tick(mv)]
                    if ci == 0 and nsi == 0:
                        tkp = em.add("g", lambda e, ot=ot, w3p=w3p: e.dma_start(
                            w3p, d_w3[:, ot * 4096:(ot + 1) * 4096]), waits=wdep,
                            inc=dg, n=16)
                        em.add("y", lambda e, ot=ot, w3p=w3p: e.dma_start(
                            d_w3b[:, ot * 4096:(ot + 1) * 4096], w3p), waits=[tkp],
                            inc=ds, n=16)
                    else:
                        tkp = em.add("y", lambda e, ot=ot, w3p=w3p: e.dma_start(
                            w3p, d_w3b[:, ot * 4096:(ot + 1) * 4096]), waits=wdep,
                            inc=ds, n=16)
                    xt_ = t32(W_XNF + (ot % 2) * 512, 512)
                    wx = [ev4[ot - 2]] if ot >= 2 else []
                    tk_x2 = em.add("y", lambda e, coff=coff, nsoff=nsoff, ot=ot, ns=ns,
                                   xt_=xt_: e.dma_start(
                                       xt_[:, 0:ns],
                                       d_xnf[:, ot * 2048 + coff + nsoff:
                                             ot * 2048 + coff + nsoff + ns]),
                                   waits=wx, inc=ds, n=16)
                    psy = P[4 + (ot % 2)]
                    def f4mm(e, w3p=w3p, nsoff=nsoff, ns=ns, psy=psy, CH=CH):
                        for kt in range(32):
                            mm = e.matmul(psy[:, 0:ns], w3p[:, kt * 128:(kt + 1) * 128],
                                          t16(F_H + kt * CH + nsoff, ns),
                                          start=(kt == 0), stop=(kt == 31))
                        return mm
                    wmm4 = [tkp, t_d_ready]
                    if ot >= 2:
                        wmm4.append(ev4[ot - 2])
                    else:
                        wmm4.append(em.tick(mv))
                    tkm = em.add("t", f4mm, waits=wmm4, inc=mt)
                    tke = em.add("v", lambda e, psy=psy, ot=ot, ns=ns, xt_=xt_:
                                 e.scalar_tensor_tensor(
                                     r_sb[:, ot * 512:ot * 512 + ns], psy[:, 0:ns],
                                     c_s3[:, ot:ot + 1], xt_[:, 0:ns],
                                     ALU.mult, ALU.add), waits=[tkm, tk_x2], inc=mv)
                    ev4[ot] = tke
                r16 = t16(F_C, 4096)
                rq16 = t16(F_C + 4096, 4096)
                cps = []
                for dt in range(8):
                    em.add("s", lambda e, dt=dt, ns=ns: e.copy(
                        r16[:, dt * 512:dt * 512 + ns], r_sb[:, dt * 512:dt * 512 + ns]),
                        waits=[ev4[dt]])
                    tkc_ = em.add("s", lambda e, dt=dt, ns=ns: e.activation(
                        rq16[:, dt * 512:dt * 512 + ns], r_sb[:, dt * 512:dt * 512 + ns],
                        AF.Square), inc=ms)
                    cps.append(tkc_)
                def ln2sum(e, ns=ns):
                    for dt in range(8):
                        e.matmul(S0[:, 0:ns], c_onesb[:], r16[:, dt * 512:dt * 512 + ns],
                                 start=(dt == 0), stop=(dt == 7))
                    for dt in range(8):
                        mm = e.matmul(S1[:, 0:ns], c_onesb[:], rq16[:, dt * 512:dt * 512 + ns],
                                      start=(dt == 0), stop=(dt == 7))
                    return mm
                tk_sum = em.add("t", ln2sum, waits=[cps[-1]], inc=mt)
                sq2 = [tk_sum]
                lnm = t32(LN_M, 512)[0:1, :]
                lnv = t32(LN_V, 512)[0:1, :]
                lnt = t32(LN_T, 512)[0:1, :]
                lns = t32(LN_S, 512)[0:1, :]
                rstd_t = t32(LN_RSTD, 512)[0:1, :]
                em.add("v", lambda e, ns=ns: e.tensor_scalar(
                    lnm[:, 0:ns], S0[:, 0:ns], 1.0 / 1024, 0.0, ALU.mult, ALU.add),
                    waits=[tk_sum, sq2[-1]])
                em.add("v", lambda e, ns=ns: e.tensor_tensor(
                    lnt[:, 0:ns], lnm[:, 0:ns], lnm[:, 0:ns], ALU.mult), drain=True)
                em.add("v", lambda e, ns=ns: e.tensor_scalar(
                    lnv[:, 0:ns], S1[:, 0:ns], 1.0 / 1024, 0.0, ALU.mult, ALU.add))
                tkv_ = em.add("v", lambda e, ns=ns: e.tensor_tensor(
                    lnv[:, 0:ns], lnv[:, 0:ns], lnt[:, 0:ns], ALU.subtract), inc=mv, drain=True)
                em.add("s", lambda e, ns=ns: e.activation(
                    lns[:, 0:ns], lnv[:, 0:ns], AF.Ln, bias=epsv[0:1, :]),
                    waits=[tkv_], drain=True)
                tkr = em.add("s", lambda e, ns=ns: e.activation(
                    rstd_t[:, 0:ns], lns[:, 0:ns], AF.Exp, scale=-0.5),
                    drain=True, inc=ms)
                def ln2b(e, ns=ns):
                    e.matmul(P[2][:, 0:ns], c_ones1[:], lnm[:, 0:ns], start=True, stop=True)
                    return e.matmul(P[3][:, 0:ns], c_ones1[:], rstd_t[:, 0:ns],
                                    start=True, stop=True)
                tkb = em.add("t", ln2b, waits=[tkr], inc=mt)
                out_sb = t32(W_XO, 4096)
                for dt in range(8):
                    em.add("v", lambda e, dt=dt, ns=ns: e.tensor_tensor(
                        out_sb[:, dt * 512:dt * 512 + ns], r_sb[:, dt * 512:dt * 512 + ns],
                        P[2][:, 0:ns], ALU.subtract), waits=[tkb])
                    tka = em.add("v", lambda e, dt=dt, ns=ns: e.tensor_tensor(
                        out_sb[:, dt * 512:dt * 512 + ns], out_sb[:, dt * 512:dt * 512 + ns],
                        P[3][:, 0:ns], ALU.mult), inc=mv)
                tko = em.add("y", lambda e, coff=coff, nsoff=nsoff, ns=ns: e.dma_start(
                    bass.AP(y_h, coff + nsoff, [[8 * 2048, 128], [2048, 8], [1, ns]]),
                    bass.AP(ph32, W_XO, [[PH32_COLS, 128], [512, 8], [1, ns]])),
                    waits=[tka], inc=ds, n=16)
                em.add("v", lambda e: e.engine_nop(), waits=[tko])
                nsoff += ns

        em.add("g", lambda e: e.memzero(c_exp[:]), waits=[em.tick(ds), em.tick(dg)])

        with nc.Block() as blk:
            def runner(engname):
                def run(e):
                    seen = {k: 0 for k in SEMS}
                    for waits, inc, n, fn, drain in em.ops[engname]:
                        for (sk, v) in waits:
                            if v > seen[sk]:
                                e.wait_ge(sems[sk], v)
                                seen[sk] = v
                        if drain:
                            e.drain()
                        ins = fn(e)
                        if inc is not None:
                            ins.then_inc(sems[inc], n)
                return run
            blk.tensor(runner("t"))
            blk.scalar(runner("s"))
            blk.vector(runner("v"))
            blk.sync(runner("y"))
            blk.gpsimd(runner("g"))

    return nc


def kernel(**inputs):
    maps, chunk_lists = prep_inputs(inputs)
    nc = build_program()
    res = run_bass_kernel_spmd(nc, maps, core_ids=list(range(8)))
    out = np.zeros((B, N, D), dtype=np.float32)
    for c in range(8):
        y = res.results[c]["y"].reshape(128, 8, 2048)
        yt = y.transpose(2, 1, 0).reshape(2048, D)
        for s, ch in enumerate(chunk_lists[c]):
            out[c % 4, ch * 512:(ch + 1) * 512, :] = yt[s * 512:(s + 1) * 512]
    return out
